# version 27 (final)
# baseline (speedup 1.0000x reference)
import math
import numpy as np

import concourse.bass as bass
import concourse.mybir as mybir
from concourse.bass_utils import run_bass_kernel_spmd

F32 = mybir.dt.float32
BF16 = mybir.dt.bfloat16
I32 = mybir.dt.int32
AF = mybir.ActivationFunctionType
ALU = mybir.AluOpType
AX = mybir.AxisListType

B, S, D = 4, 2048, 2048
H, D_NOPE, D_ROPE, D_V = 16, 128, 64, 128
R_Q, R_KV = 1536, 512
D_FF = 8192
EPS = 1e-5
ATTN_SCALE = float(np.float32(1.0) / np.sqrt(np.float32(D_NOPE + D_ROPE)))
V_SCALE = float(np.float32(1.0) / np.sqrt(np.float32(H * D_V)))
QMAX, QMIN = 32767.0, -32768.0
NEG = -1.0e30

NT = 16
OWN = [[0, 3, 4, 7, 8, 11, 12, 15],
       [1, 2, 5, 6, 9, 10, 13, 14]]
SLIST = [1, 1, 2, 2, 3, 3, 4, 4]

ENGS = ("tensor", "vector", "scalar", "sync", "gpsimd")
ESEM = {"tensor": "T", "vector": "V", "scalar": "S", "sync": "Y", "gpsimd": "G"}


class Buf:
    __slots__ = ("w", "readers")

    def __init__(self):
        self.w = None
        self.readers = {}


class Prog:
    def __init__(self, nc):
        self.nc = nc
        self.q = {e: [] for e in ENGS}
        self.sems = {}
        self.cnt = {}
        self.last_wait = {e: {} for e in ENGS}

    def sem(self, name):
        if name not in self.sems:
            self.sems[name] = self.nc.alloc_semaphore(name="sem_" + name)
            self.cnt[name] = 0
        return self.sems[name]

    def _wait(self, eng, sname, val):
        if val <= 0:
            return
        assert val <= self.cnt.get(sname, 0), (eng, sname, val, self.cnt.get(sname))
        lw = self.last_wait[eng]
        if lw.get(sname, 0) >= val:
            return
        lw[sname] = val
        h = self.sem(sname)
        self.q[eng].append(lambda e, h=h, v=val: e.wait_ge(h, v))

    def do(self, eng, build, reads=(), writes=(), dma=None, accum=False):
        my = ESEM[eng]
        for b in reads:
            if b.w:
                self._wait(eng, *b.w)
        for b in writes:
            if b.readers:
                for sn, v in b.readers.items():
                    self._wait(eng, sn, v)
            elif b.w and not accum:
                self._wait(eng, *b.w)
        sname = dma if dma else my
        inc = 16 if dma else 1
        h = self.sem(sname)
        self.cnt[sname] = self.cnt.get(sname, 0) + inc
        n = self.cnt[sname]
        self.q[eng].append(lambda e, b=build, h=h, i=inc: b(e).then_inc(h, i))
        for b in reads:
            if b.readers.get(sname, 0) < n:
                b.readers[sname] = n
        for b in writes:
            b.w = (sname, n)
            b.readers = {}
        return n

    def raw_wait(self, eng, sname, val):
        self._wait(eng, sname, val)

    def barrier(self, tag):
        vals = {}
        for e in ENGS:
            self._wait(e, ESEM[e], self.cnt.get(ESEM[e], 0))
            if e == "gpsimd":
                for sn in list(self.cnt):
                    if sn.startswith("ST") or sn == "GD" or sn.startswith("GL"):
                        self._wait(e, sn, self.cnt.get(sn, 0))
            if e == "sync":
                for sn in list(self.cnt):
                    if sn.startswith("L") or sn.startswith("SY"):
                        self._wait(e, sn, self.cnt.get(sn, 0))
            sn = "BAR" + ESEM[e]
            h = self.sem(sn)
            self.cnt[sn] = self.cnt.get(sn, 0) + 1
            vals[sn] = self.cnt[sn]
            self.q[e].append(lambda eng, h=h: eng.sem_inc(h, 1))
        for e in ENGS:
            for sn, v in vals.items():
                if sn != "BAR" + ESEM[e]:
                    self._wait(e, sn, v)

    def emit(self, block):
        for e in ENGS:
            ops = self.q[e]
            dec = getattr(block, e)

            def make(ops):
                def body(engine):
                    for f in ops:
                        f(engine)
                return body
            dec(make(ops))


def build_program(debug=(), stop_after=None):
    nc = bass.Bass("TRN2", target_bir_lowering=False)

    xq_d = nc.dram_tensor("xq", [8 * 128, D], F32, kind="ExternalInput")
    xn_d = nc.dram_tensor("xnat", [S, D], F32, kind="ExternalInput")
    lm_d = nc.dram_tensor("lastmask", [8, 128, 512], F32, kind="ExternalInput")
    cst_d = nc.dram_tensor("consts", [128, 576], F32, kind="ExternalInput")
    w_cq_d = nc.dram_tensor("wcqs", [128, 16 * 1536], F32, kind="ExternalInput")
    w_dqn_d = nc.dram_tensor("wdqns", [H, 128, 12 * 128], F32, kind="ExternalInput")
    w_dqr_d = nc.dram_tensor("wdqrs", [H, 128, 12 * 64], F32, kind="ExternalInput")
    w_ckv_d = nc.dram_tensor("wckvs", [128, 16 * 512], F32, kind="ExternalInput")
    w_dkn_d = nc.dram_tensor("wdkns", [H, 128, 4 * 128], F32, kind="ExternalInput")
    w_dv_d = nc.dram_tensor("wdvs4", [4, 128, 4 * 512], F32, kind="ExternalInput")
    w_kr_d = nc.dram_tensor("wkrs", [128, 16 * 64], F32, kind="ExternalInput")
    w_pr_d = nc.dram_tensor("wprs", [128, 16 * 2048], F32, kind="ExternalInput")
    w1t_d = nc.dram_tensor("w1s", [64, 128, 16 * 128], F32, kind="ExternalInput")
    w2t_d = nc.dram_tensor("w2t", [D_FF, D], F32, kind="ExternalInput")
    out_d = nc.dram_tensor("out", [8 * 128, D], F32, kind="ExternalOutput")
    dbg_d = {}
    for name, shape in debug:
        dbg_d[name] = nc.dram_tensor("dbg_" + name, shape, F32, kind="ExternalOutput")

    cqB = nc.dram_tensor("cqB", [2, 128, 16 * 1536], BF16)
    dqnB = nc.dram_tensor("dqnB", [2, H, 128, 12 * 128], BF16)
    dqrB = nc.dram_tensor("dqrB", [2, H, 128, 12 * 64], BF16)
    ckvB = nc.dram_tensor("ckvB", [2, 128, 16 * 512], BF16)
    dknB = nc.dram_tensor("dknB", [2, H, 128, 4 * 128], BF16)
    dvB = nc.dram_tensor("dvB", [2, 4, 128, 4 * 512], BF16)
    krB = nc.dram_tensor("krB", [2, 128, 16 * 64], BF16)
    prB = nc.dram_tensor("prB", [2, 128, 16 * 2048], BF16)
    w1B = nc.dram_tensor("w1B", [64, 128, 16 * 128], BF16)
    w2B = nc.dram_tensor("w2B", [D_FF, D], BF16)
    kkB = nc.dram_tensor("kkB", [2, H, 128, 2048], BF16)
    vvB = nc.dram_tensor("vvB", [2, 4, 128, 16 * 512], BF16)
    krotB = nc.dram_tensor("krotB", [2, H, 64, 2048], BF16)

    HCOLS = 22272
    BCOLS = 60672
    heap = nc.alloc_sbuf_tensor("heap", [128, HCOLS], F32)
    heapb = nc.alloc_sbuf_tensor("heapb", [128, BCOLS], BF16)
    int_t = nc.alloc_sbuf_tensor("int_t", [128, 512], I32)

    off = {}
    def region(name, cols):
        off[name] = region.p
        region.p += cols
    region.p = 0
    region("consts", 576)
    region("stats", 192)
    region("FP1", 4096)
    region("FPG", 8192)
    region("FPC", 6144)
    region("WS", 3072)
    assert region.p <= HCOLS, region.p

    offb = {}
    def regionb(name, cols):
        offb[name] = regionb.p
        regionb.p += cols
    regionb.p = 0
    regionb("XN", 16384)
    regionb("NQ", 12288)
    regionb("AB", 32000)
    assert regionb.p <= BCOLS, regionb.p

    def hs(name, r0, r1, c0, c1):
        return heap[r0:r1, off[name] + c0: off[name] + c1]

    def hv(name, c0, c1):
        return heap[:, off[name] + c0: off[name] + c1]

    def bs(name, r0, r1, c0, c1):
        return heapb[r0:r1, offb[name] + c0: offb[name] + c1]

    def bv(name, c0, c1):
        return heapb[:, offb[name] + c0: offb[name] + c1]

    C_ID, C_ONES, C_SWAP, C_GQ, C_GKV, C_COS, C_SIN, C_SGN, C_SPM, C_EPS, C_M30 = \
        0, 128, 256, 384, 396, 400, 416, 432, 433, 449, 450
    ident = hv("consts", C_ID, C_ID + 128)
    onesm = hv("consts", C_ONES, C_ONES + 128)
    swap64 = hs("consts", 0, 64, C_SWAP, C_SWAP + 64)

    ST_BN24, ST_BN2, ST_U, ST_MR, ST_MX, ST_DEN, ST_RC, ST_SCQ = 0, 24, 26, 27, 28, 36, 44, 48

    F_KROPE, F_KSWAP = 0, 2048
    F_Z, F_KF, F_GH, F_GL = 0, 512, 1024, 1536
    F_PROBS, F_QRF, F_QSF, F_QOF = 0, 2048, 2560, 3072
    F_SQ = 3584
    C_XT0, C_XT1, C_XNTM = 0, 2048, 4096
    C_KROTF = 0
    C_MASK = 0
    C_XTOK = 2048
    A_XTP = 0
    A_KRP = 16384
    A_DKN, A_DV4, A_KP, A_VP, A_KROTP = 6144, 8192, 12288, 16384, 17408
    A_AP, A_W1, A_W2 = 0, 8192, 14336
    A_CQ = 0
    A_K, A_V, A_KROT, A_QN, A_QROT, A_PT, A_DQN, A_DQR = \
        0, 8192, 16384, 24576, 25600, 26624, 27392, 30464
    A_PR = 0

    banks = [nc.alloc_psum_tensor("pb%d" % i, [128, 512], F32) for i in range(8)]

    p = Prog(nc)
    bufs = {}

    def BF(name):
        if name not in bufs:
            bufs[name] = Buf()
        return bufs[name]
    pbank = [BF("psum%d" % i) for i in range(8)]

    LOAD_SLOTS = ["L%d" % i for i in range(7)]
    slot_i = [0]

    def next_slot():
        sl = LOAD_SLOTS[slot_i[0] % len(LOAD_SLOTS)]
        slot_i[0] += 1
        return sl

    def load(dst_ap, src_ap, dstbuf, srcextra=(), slot=None, extra_writes=()):
        sl = slot if slot else next_slot()
        p.raw_wait("sync", sl, p.cnt.get(sl, 0))
        return p.do("sync", lambda e, o=dst_ap, i=src_ap: e.dma_start(out=o, in_=i),
                    reads=list(srcextra), writes=[dstbuf] + list(extra_writes), dma=sl)

    gl_i = [0]

    def gload(dst_ap, src_ap, dstbuf, srcextra=()):
        sem = "GL%d" % (gl_i[0] % 3)
        gl_i[0] += 1
        p.raw_wait("gpsimd", sem, p.cnt.get(sem, 0))
        return p.do("gpsimd", lambda e, o=dst_ap, i=src_ap: e.dma_start(out=o, in_=i),
                    reads=list(srcextra), writes=[dstbuf], dma=sem)

    st_i = [0]

    def store(dst_ap, src_ap, srcbuf, q="gpsimd", sem=None):
        if sem is None:
            sem = ("ST%d" % (st_i[0] % 4)) if q == "gpsimd" else ("SY%d" % (st_i[0] % 4))
            st_i[0] += 1
        p.raw_wait(q, sem, p.cnt.get(sem, 0))
        return p.do(q, lambda e, o=dst_ap, i=src_ap: e.dma_start(out=o, in_=i),
                    reads=[srcbuf], writes=[], dma=sem)

    def tap(name, src_ap, srcbuf):
        if name in dbg_d:
            store(dbg_d[name][:, :], src_ap, srcbuf)
            for sn in list(p.cnt):
                if sn.startswith("ST"):
                    p.raw_wait("gpsimd", sn, p.cnt.get(sn, 0))

    def stop_here(tag):
        if stop_after == tag:
            for sn in list(p.cnt):
                if sn.startswith("ST"):
                    p.raw_wait("gpsimd", sn, p.cnt.get(sn, 0))
                if sn.startswith("SY"):
                    p.raw_wait("sync", sn, p.cnt.get(sn, 0))
            with nc.Block() as block:
                p.emit(block)
            return True
        return False

    cstb = BF("consts")
    load(hv("consts", 0, 576), cst_d[:, :], cstb)
    sgn_ap = hs("consts", 0, 64, C_SGN, C_SGN + 1)
    p.do("vector", lambda e: e.tensor_scalar_mul(
        hs("consts", 0, 64, C_SPM, C_SPM + 16), hs("consts", 0, 64, C_SIN, C_SIN + 16), sgn_ap),
        reads=[cstb], writes=[cstb])

    wsplit_f = BF("wsplit_f")
    wsplit_h = BF("wsplit_h")
    wsplit_l = BF("wsplit_l")

    wsplit_f2 = BF("wsplit_f2"); wsplit_h2 = BF("wsplit_h2"); wsplit_l2 = BF("wsplit_l2")
    split_par = [0]

    def split_weight(src_fp32, dst_hi, dst_lo, ncols, late=False):
        par = split_par[0] % 2
        split_par[0] += 1
        assert ncols <= 1536
        fb = wsplit_f if par == 0 else wsplit_f2
        hb = wsplit_h if par == 0 else wsplit_h2
        lb = wsplit_l if par == 0 else wsplit_l2
        ao = 27904
        fa = hv("WS", 1536 * par, 1536 * par + ncols)
        hi = bv("AB", ao + 2048 * par, ao + 2048 * par + ncols)
        lo = bv("AB", ao - 4096 + 2048 * par, ao - 4096 + 2048 * par + ncols)
        load(fa, src_fp32, fb)
        gload(hi, src_fp32, hb)
        eng = "gpsimd" if par == 0 else "vector"
        p.do(eng, lambda e, fa=fa, hi=hi, lo=lo: e.tensor_tensor(lo, fa, hi, ALU.subtract),
             reads=[fb, hb], writes=[lb])
        store(dst_lo, lo, lb)

    def cast_weight(src_fp32, dst_bf16):
        p.do("gpsimd", lambda e, o=dst_bf16, i=src_fp32: e.dma_start(out=o, in_=i),
             reads=[], writes=[], dma="GD")

    deferred_splits = []
    p1_splits_pending = []
    hi_casts = []
    cast_weight(w_ckv_d[:, :], ckvB[0])
    cast_weight(w_kr_d[:, :], krB[0])
    for kc in range(16):
        deferred_splits.append((w_cq_d[:, 1536 * kc:1536 * (kc + 1)],
                                None, cqB[1, :, 1536 * kc:1536 * (kc + 1)], 1536))
    for h in range(H):
        deferred_splits.append((w_dqn_d[h], None, dqnB[1, h], 1536))
        deferred_splits.append((w_dqr_d[h], None, dqrB[1, h], 768))
        p1_splits_pending.append((w_dkn_d[h], None, dknB[1, h], 512))
    for hg in range(4):
        for j2 in range(2):
            p1_splits_pending.append((w_dv_d[hg][:, 1024 * j2:1024 * (j2 + 1)],
                                      None, dvB[1, hg][:, 1024 * j2:1024 * (j2 + 1)], 1024))
    for kc in range(16):
        for j2 in range(2):
            deferred_splits.append((w_pr_d[:, 2048 * kc + 1024 * j2:2048 * kc + 1024 * (j2 + 1)],
                                    None, prB[1, :, 2048 * kc + 1024 * j2:2048 * kc + 1024 * (j2 + 1)], 1024))
    for c2 in range(8):
        split_weight(w_ckv_d[:, 1024 * c2:1024 * (c2 + 1)],
                     None, ckvB[1, :, 1024 * c2:1024 * (c2 + 1)], 1024)
    split_weight(w_kr_d[:, :], None, krB[1], 1024)
    p.barrier("Pm1")
    cast_weight(w_dkn_d[:, :, :], dknB[0])
    cast_weight(w_dv_d[:, :, :], dvB[0])

    def cast_gx(src_fp32, dst_bf16, sem):
        p.do("gpsimd", lambda e, o=dst_bf16, i=src_fp32: e.dma_start(out=o, in_=i),
             reads=[], writes=[], dma=sem)

    gx_p1 = []
    for fg in range(0, 64, 4):
        gx_p1.append((w1t_d[fg:fg + 4], w1B[fg:fg + 4], "GXA" if fg < 32 else "GXB"))
    gx_p1end = []
    for i2 in range(16):
        r2 = 512 * i2
        gx_p1end.append((w2t_d[r2:r2 + 512, :], w2B[r2:r2 + 512, :], "GXC" if i2 < 8 else "GXD"))
    gx_p2a = []
    for i2 in range(4):
        gx_p2a.append((w_cq_d[32 * i2:32 * (i2 + 1), :], cqB[0, 32 * i2:32 * (i2 + 1), :], "GXE"))
    for h2 in range(0, 16, 4):
        gx_p2a.append((w_dqn_d[h2:h2 + 4], dqnB[0, h2:h2 + 4], "GXE"))
    gx_p2a.append((w_dqr_d[0:8], dqrB[0, 0:8], "GXE"))
    gx_p2a.append((w_dqr_d[8:16], dqrB[0, 8:16], "GXE"))
    gx_p2b = []
    for i2 in range(8):
        gx_p2b.append((w_pr_d[16 * i2:16 * (i2 + 1), :], prB[0, 16 * i2:16 * (i2 + 1), :], "GXF"))
    if stop_here("PW"):
        return nc

    def mm3w(bank_ap, bankbuf, wh_ap, wl_ap, xh_ap, xl_ap, wbufs, xbufs, first, last, n=None):
        p.do("tensor", lambda e: e.matmul(bank_ap, wh_ap, xh_ap, start=first, stop=False),
             reads=list(wbufs) + list(xbufs), writes=[pbank[n]], accum=not first)
        p.do("tensor", lambda e: e.matmul(bank_ap, wh_ap, xl_ap, start=False, stop=False),
             reads=list(wbufs) + list(xbufs), writes=[pbank[n]], accum=True)
        p.do("tensor", lambda e: e.matmul(bank_ap, wl_ap, xh_ap, start=False, stop=last),
             reads=list(wbufs) + list(xbufs), writes=[pbank[n]], accum=True)

    def evac_pair(bank_ap, nb, hi_ap, lo_ap, hibuf, lobuf, scale=None):
        if scale is None:
            p.do("scalar", lambda e: e.activation(hi_ap, bank_ap, AF.Copy),
                 reads=[pbank[nb]], writes=[hibuf])
            p.do("vector", lambda e: e.tensor_tensor(lo_ap, bank_ap, hi_ap, ALU.subtract),
                 reads=[pbank[nb], hibuf], writes=[lobuf])
        else:
            p.do("scalar", lambda e: e.activation(hi_ap, bank_ap, AF.Copy, scale=scale),
                 reads=[pbank[nb]], writes=[hibuf])
            p.do("vector", lambda e: e.scalar_tensor_tensor(
                lo_ap, bank_ap, scale, hi_ap, ALU.mult, ALU.subtract),
                reads=[pbank[nb], hibuf], writes=[lobuf])

    xt = [[BF("xt%d_%d" % (par, c)) for c in range(4)] for par in range(2)]
    xntmp = BF("xntmp")
    xtpb = BF("xtp")
    nkvf = BF("nkvf")
    nkvb = BF("nkvb")
    krop = BF("krope"); kswb = BF("kswap"); statb = BF("stats")
    ckvb = BF("ckvb"); krb = BF("krb")
    sskvb = BF("sskv")

    statp = [BF("stats0"), BF("stats1")]
    xntmp2 = BF("xntmp2")
    xnt_q = [BF("xntq%d" % c) for c in range(4)]
    tb_rot = [0]
    TBANKS = [0, 1, 2, 5, 6, 7]

    def layernorm_tile(xcols, xbuf, hi_fn, lo_fn, dsthi, dstlo, par=0):
        xa = hv("FPC", xcols, xcols + 2048)
        SO = 96 * par
        sb = statp[par]
        xnt = hv("FPC", C_XNTM, C_XNTM + 2048)
        xntb = xntmp
        for c4 in range(4):
            p.do("vector", lambda e, c4=c4, xa=xa, SO=SO: e.bn_stats(
                hv("stats", SO + ST_BN24 + 6 * c4, SO + ST_BN24 + 6 * c4 + 6), xa[:, 512 * c4:512 * (c4 + 1)]),
                reads=[xbuf[c4]], writes=[sb])
        p.do("vector", lambda e, SO=SO: e.bn_aggr(hv("stats", SO + ST_BN2, SO + ST_BN2 + 2),
                                           hv("stats", SO + ST_BN24, SO + ST_BN24 + 24)),
             reads=[sb], writes=[sb])
        p.do("scalar", lambda e, SO=SO: e.activation(hv("stats", SO + ST_U, SO + ST_U + 1),
                                              hv("stats", SO + ST_BN2 + 1, SO + ST_BN2 + 2),
                                              AF.Sqrt, bias=hv("consts", C_EPS, C_EPS + 1),
                                              scale=1.0),
             reads=[sb, cstb], writes=[sb])
        p.do("vector", lambda e, SO=SO: e.reciprocal(hv("stats", SO + ST_U, SO + ST_U + 1),
                                              hv("stats", SO + ST_U, SO + ST_U + 1)),
             reads=[sb], writes=[sb])
        p.do("vector", lambda e, SO=SO: e.tensor_scalar(
            hv("stats", SO + ST_MR, SO + ST_MR + 1), hv("stats", SO + ST_BN2, SO + ST_BN2 + 1),
            hv("stats", SO + ST_U, SO + ST_U + 1), -1.0, ALU.mult, ALU.mult),
            reads=[sb], writes=[sb])
        for c in range(4):
            p.do("scalar", lambda e, xa=xa, SO=SO, xnt=xnt, c=c: e.activation(
                xnt[:, 512 * c:512 * (c + 1)], xa[:, 512 * c:512 * (c + 1)], AF.Identity,
                bias=hv("stats", SO + ST_MR, SO + ST_MR + 1), scale=hv("stats", SO + ST_U, SO + ST_U + 1)),
                reads=[xbuf[c], sb], writes=[xnt_q[c]])
        for c in range(4):
            bk = TBANKS[tb_rot[0] % 6]
            tb_rot[0] += 1
            for q in range(4):
                p.do("tensor", lambda e, c=c, q=q, bk=bk, xnt=xnt: e.transpose(
                    banks[bk][:, 128 * q:128 * (q + 1)],
                    xnt[:, 512 * c + 128 * q:512 * c + 128 * (q + 1)], ident),
                    reads=[xnt_q[c], cstb], writes=[pbank[bk]], accum=(q > 0))
            evac_pair(banks[bk][:, 0:512], bk, hi_fn(c), lo_fn(c), dsthi, dstlo)

    _s = next_slot()
    load(bv("XN", 0, 8192), ckvB[0], ckvb, slot=_s)
    load(bv("XN", 8192, 16384), ckvB[1], ckvb, slot=_s)
    _s = next_slot()
    load(bv("AB", A_KRP, A_KRP + 1024), krB[0], krb, slot=_s)
    load(bv("AB", A_KRP + 1024, A_KRP + 2048), krB[1], krb, slot=_s)

    p1_splits = list(p1_splits_pending)
    for grp in range(4):
        for j in range(4):
            t = grp * 4 + j
            xb = xt[t % 2]
            for c in range(4):
                load(hv("FPC", 2048 * (t % 2) + 512 * c, 2048 * (t % 2) + 512 * (c + 1)),
                     xn_d[128 * t:128 * (t + 1), 512 * c:512 * (c + 1)], xb[c])
            layernorm_tile(2048 * (t % 2), xb,
                           lambda c, j=j: bv("AB", A_XTP + 2048 * c, A_XTP + 2048 * (c + 1))
                           .rearrange("p (k m) -> p k m", m=512)[:, :, 128 * j:128 * (j + 1)],
                           lambda c, j=j: bv("AB", A_XTP + 8192 + 2048 * c, A_XTP + 8192 + 2048 * (c + 1))
                           .rearrange("p (k m) -> p k m", m=512)[:, :, 128 * j:128 * (j + 1)],
                           xtpb, xtpb, par=t % 2)
            if t >= 4 and gx_p1:
                sx, dx, semx = gx_p1.pop(0)
                cast_gx(sx, dx, semx)
        for mc in range(4):
            bk = mc
            for kc in range(16):
                mm3w(banks[bk][:, 0:512], None,
                     bv("XN", 512 * kc + 128 * mc, 512 * kc + 128 * (mc + 1)),
                     bv("XN", 8192 + 512 * kc + 128 * mc, 8192 + 512 * kc + 128 * (mc + 1)),
                     bv("AB", A_XTP + 512 * kc, A_XTP + 512 * (kc + 1)),
                     bv("AB", A_XTP + 8192 + 512 * kc, A_XTP + 8192 + 512 * (kc + 1)),
                     [ckvb], [xtpb], kc == 0, kc == 15, n=bk)
            p.do("vector", lambda e, mc=mc, grp=grp, bk=bk: e.tensor_copy(
                hv("FPG", 2048 * mc + 512 * grp, 2048 * mc + 512 * (grp + 1)), banks[bk][:, 0:512]),
                reads=[pbank[bk]], writes=[nkvf])
            p.do("scalar", lambda e, mc=mc, grp=grp: e.activation(
                hv("FPC", C_XNTM, C_XNTM + 512),
                hv("FPG", 2048 * mc + 512 * grp, 2048 * mc + 512 * (grp + 1)), AF.Square),
                reads=[nkvf], writes=[xntmp])
            p.do("tensor", lambda e, mc=mc: e.matmul(
                banks[6][0:1, 0:512], hv("consts", C_ONES, C_ONES + 1),
                hv("FPC", C_XNTM, C_XNTM + 512),
                start=(mc == 0), stop=(mc == 3)),
                reads=[xntmp, cstb], writes=[pbank[6]], accum=(mc != 0))
        p.do("vector", lambda e, grp=grp: e.tensor_copy(
            hs("FP1", 64, 65, F_KROPE + 512 * grp, F_KROPE + 512 * (grp + 1)), banks[6][0:1, 0:512]),
            reads=[pbank[6]], writes=[sskvb])
        for kc in range(16):
            p.do("tensor", lambda e, kc=kc: e.matmul(
                banks[4][0:64, 0:512], bv("AB", A_KRP + 64 * kc, A_KRP + 64 * (kc + 1)),
                bv("AB", A_XTP + 512 * kc, A_XTP + 512 * (kc + 1)),
                start=(kc == 0), stop=False),
                reads=[krb, xtpb], writes=[pbank[4]], accum=(kc != 0))
            p.do("tensor", lambda e, kc=kc: e.matmul(
                banks[4][0:64, 0:512], bv("AB", A_KRP + 64 * kc, A_KRP + 64 * (kc + 1)),
                bv("AB", A_XTP + 8192 + 512 * kc, A_XTP + 8192 + 512 * (kc + 1)),
                start=False, stop=False),
                reads=[krb, xtpb], writes=[pbank[4]], accum=True)
            p.do("tensor", lambda e, kc=kc: e.matmul(
                banks[4][0:64, 0:512], bv("AB", A_KRP + 1024 + 64 * kc, A_KRP + 1024 + 64 * (kc + 1)),
                bv("AB", A_XTP + 512 * kc, A_XTP + 512 * (kc + 1)),
                start=False, stop=(kc == 15)),
                reads=[krb, xtpb], writes=[pbank[4]], accum=True)
        p.do("vector", lambda e, grp=grp: e.tensor_copy(
            hs("FP1", 0, 64, F_KROPE + 512 * grp, F_KROPE + 512 * (grp + 1)), banks[4][0:64, 0:512]),
            reads=[pbank[4]], writes=[krop])
        for _ in range(6):
            if p1_splits:
                sw = p1_splits.pop(0)
                split_weight(sw[0], sw[1], sw[2], sw[3], late=True)

    p.do("scalar", lambda e: e.activation(hs("FP1", 64, 65, 0, 2048), hs("FP1", 64, 65, 0, 2048),
                                          AF.Sqrt, bias=hs("consts", 64, 65, C_EPS, C_EPS + 1),
                                          scale=1.0 / R_KV),
         reads=[sskvb, cstb], writes=[sskvb])
    p.do("vector", lambda e: e.reciprocal(hs("FP1", 64, 65, 0, 2048), hs("FP1", 64, 65, 0, 2048)),
         reads=[sskvb], writes=[sskvb])
    for c4 in range(4):
        p.do("tensor", lambda e, c4=c4: e.matmul(
            banks[7][:, 0:512], hs("consts", 64, 65, C_ONES, C_ONES + 128), hs("FP1", 64, 65, 512 * c4, 512 * (c4 + 1)),
            start=True, stop=True),
            reads=[sskvb, cstb], writes=[pbank[7]])
        for mc in range(4):
            p.do("vector", lambda e, c4=c4, mc=mc: e.tensor_tensor(
                hv("FPG", 2048 * mc + 512 * c4, 2048 * mc + 512 * (c4 + 1)),
                hv("FPG", 2048 * mc + 512 * c4, 2048 * mc + 512 * (c4 + 1)),
                banks[7][:, 0:512], ALU.mult),
                reads=[pbank[7], nkvf], writes=[nkvf])
    for mc in range(4):
        p.do("scalar", lambda e, mc=mc: e.activation(
            bv("XN", 2048 * mc, 2048 * (mc + 1)), hv("FPG", 2048 * mc, 2048 * (mc + 1)),
            AF.Copy, scale=hv("consts", C_GKV + mc, C_GKV + mc + 1)),
            reads=[nkvf, cstb], writes=[nkvb])
        p.do("vector", lambda e, mc=mc: e.scalar_tensor_tensor(
            bv("XN", 8192 + 2048 * mc, 8192 + 2048 * (mc + 1)),
            hv("FPG", 2048 * mc, 2048 * (mc + 1)),
            hv("consts", C_GKV + mc, C_GKV + mc + 1),
            bv("XN", 2048 * mc, 2048 * (mc + 1)),
            ALU.mult, ALU.subtract),
            reads=[nkvf, nkvb, cstb], writes=[nkvb])
    for c4 in range(4):
        p.do("tensor", lambda e, c4=c4: e.matmul(
            banks[5][0:64, 0:512], swap64, hs("FP1", 0, 64, F_KROPE + 512 * c4, F_KROPE + 512 * (c4 + 1)),
            start=True, stop=True),
            reads=[krop, cstb], writes=[pbank[5]])
        p.do("vector", lambda e, c4=c4: e.tensor_copy(
            hs("FP1", 0, 64, F_KSWAP + 512 * c4, F_KSWAP + 512 * (c4 + 1)), banks[5][0:64, 0:512]),
            reads=[pbank[5]], writes=[kswb])
    p.barrier("P1")
    if stop_here("P1"):
        return nc

    dknp = [BF("dknp0"), BF("dknp1")]; dv4p = [BF("dv4p0"), BF("dv4p1")]
    kpb = BF("kpb"); vpb = BF("vpb"); krotf = BF("krotf"); krotp = BF("krotp")

    def load_dkn(h):
        pr2 = h % 2
        _s = next_slot()
        load(bv("AB", A_DKN + 1024 * pr2, A_DKN + 1024 * pr2 + 512), dknB[0, h], dknp[pr2], slot=_s)
        load(bv("AB", A_DKN + 1024 * pr2 + 512, A_DKN + 1024 * pr2 + 1024), dknB[1, h], dknp[pr2], slot=_s)

    def dv_base(hg):
        return A_DV4 if hg % 2 == 0 else 0

    def load_dv(hg):
        b0 = dv_base(hg)
        _s = next_slot()
        load(bv("AB", b0, b0 + 2048), dvB[0, hg], dv4p[hg % 2], slot=_s)
        load(bv("AB", b0 + 2048, b0 + 4096), dvB[1, hg], dv4p[hg % 2], slot=_s)

    load_dkn(0)
    load_dv(0)
    for h in range(H):
        pr2 = h % 2
        if h + 1 < H:
            load_dkn(h + 1)
        if h % 4 == 0 and h + 4 < H:
            load_dv(h // 4 + 1)
        for ng in range(4):
            bk = ng % 2
            for kc in range(4):
                mm3w(banks[bk][:, 0:512], None,
                     bv("AB", A_DKN + 1024 * pr2 + 128 * kc, A_DKN + 1024 * pr2 + 128 * (kc + 1)),
                     bv("AB", A_DKN + 1024 * pr2 + 512 + 128 * kc, A_DKN + 1024 * pr2 + 512 + 128 * (kc + 1)),
                     bv("XN", 2048 * kc + 512 * ng, 2048 * kc + 512 * (ng + 1)),
                     bv("XN", 8192 + 2048 * kc + 512 * ng, 8192 + 2048 * kc + 512 * (ng + 1)),
                     [dknp[pr2]], [nkvb], kc == 0, kc == 3, n=bk)
            evac_pair(banks[bk][:, 0:512], bk,
                      bv("AB", A_KP + 512 * ng, A_KP + 512 * (ng + 1)),
                      bv("AB", A_KP + 2048 + 512 * ng, A_KP + 2048 + 512 * (ng + 1)),
                      kpb, kpb)
        store(kkB[0, h], bv("AB", A_KP, A_KP + 2048), kpb, q="sync")
        store(kkB[1, h], bv("AB", A_KP + 2048, A_KP + 4096), kpb)
        if h % 4 == 0:
            hg = h // 4
            for kt in range(16):
                bk = 2 + kt % 2
                for kc in range(4):
                    b0 = dv_base(hg)
                    mm3w(banks[bk][:, 0:512], None,
                         bv("XN", 2048 * kc + 128 * kt, 2048 * kc + 128 * (kt + 1)),
                         bv("XN", 8192 + 2048 * kc + 128 * kt, 8192 + 2048 * kc + 128 * (kt + 1)),
                         bv("AB", b0 + 512 * kc, b0 + 512 * (kc + 1)),
                         bv("AB", b0 + 2048 + 512 * kc, b0 + 2048 + 512 * (kc + 1)),
                         [nkvb], [dv4p[hg % 2]], kc == 0, kc == 3, n=bk)
                evac_pair(banks[bk][:, 0:512], bk,
                          bv("AB", A_VP, A_VP + 512),
                          bv("AB", A_VP + 512, A_VP + 1024),
                          vpb, vpb, scale=V_SCALE)
                store(vvB[0, hg][:, 512 * kt:512 * (kt + 1)], bv("AB", A_VP, A_VP + 512),
                      vpb, q="sync")
                store(vvB[1, hg][:, 512 * kt:512 * (kt + 1)], bv("AB", A_VP + 512, A_VP + 1024), vpb)
        cos_h = hs("consts", 0, 64, C_COS + h, C_COS + h + 1)
        spm_h = hs("consts", 0, 64, C_SPM + h, C_SPM + h + 1)
        p.do("vector", lambda e, cos_h=cos_h: e.tensor_scalar_mul(
            hs("FPC", 0, 64, C_KROTF, C_KROTF + 2048), hs("FP1", 0, 64, F_KROPE, F_KROPE + 2048), cos_h),
            reads=[krop, cstb], writes=[krotf])
        p.do("vector", lambda e, spm_h=spm_h: e.scalar_tensor_tensor(
            hs("FPC", 0, 64, C_KROTF, C_KROTF + 2048), hs("FP1", 0, 64, F_KSWAP, F_KSWAP + 2048), spm_h,
            hs("FPC", 0, 64, C_KROTF, C_KROTF + 2048), ALU.mult, ALU.add),
            reads=[kswb, cstb, krotf], writes=[krotf])
        p.do("scalar", lambda e: e.activation(
            bs("AB", 0, 64, A_KROTP, A_KROTP + 2048), hs("FPC", 0, 64, C_KROTF, C_KROTF + 2048),
            AF.Copy),
            reads=[krotf], writes=[krotp])
        p.do("vector", lambda e: e.tensor_tensor(
            bs("AB", 0, 64, A_KROTP + 2048, A_KROTP + 4096),
            hs("FPC", 0, 64, C_KROTF, C_KROTF + 2048),
            bs("AB", 0, 64, A_KROTP, A_KROTP + 2048), ALU.subtract),
            reads=[krotf, krotp], writes=[krotp])
        store(krotB[0, h], bs("AB", 0, 64, A_KROTP, A_KROTP + 2048), krotp, q="sync")
        store(krotB[1, h], bs("AB", 0, 64, A_KROTP + 2048, A_KROTP + 4096), krotp)
        if gx_p1:
            sx, dx, semx = gx_p1.pop(0)
            cast_gx(sx, dx, semx)
        elif gx_p1end and h < 12:
            sx, dx, semx = gx_p1end.pop(0)
            cast_gx(sx, dx, semx)
    p.barrier("P1end")
    if stop_here("P1end"):
        return nc

    xnown = BF("xn_own")
    outfm = BF("out_fm")
    nqb = BF("nq")
    gb = BF("g")
    w1buf = [BF("w1b%d" % i) for i in range(3)]
    w2buf = [BF("w2b%d" % i) for i in range(3)]
    cqbuf = [BF("cqb%d" % i) for i in range(3)]
    prbuf = [BF("prb%d" % i) for i in range(3)]
    rndb = BF("rnd")
    aslab_h = [BF("a_h%d" % i) for i in range(8)]
    aslab_l = [BF("a_l%d" % i) for i in range(8)]
    maskb = BF("maskb")
    kvb = [BF("kv0"), BF("kv1")]
    dqp = BF("dqp")
    qnb = BF("qnb"); qrf = BF("qrf"); qrotb = BF("qrotb")
    probs = BF("probs"); ptb = [BF("pt%d" % i) for i in range(3)]
    sgb = BF("sg"); obuf = BF("obuf"); xtok = BF("xtok")
    rqrow = BF("rqrow"); scqb = BF("scq")

    for g in range(2):
        slots = [g * 4 + i for i in range(4)]
        svals = [SLIST[sl] for sl in slots]

        for i in range(4):
            sl = g * 4 + i
            xb = xt[i % 2]
            for c in range(4):
                load(hv("FPC", 2048 * (i % 2) + 512 * c, 2048 * (i % 2) + 512 * (c + 1)),
                     xq_d[128 * sl:128 * (sl + 1), 512 * c:512 * (c + 1)], xb[c])
            layernorm_tile(2048 * (i % 2), xb,
                           lambda c, i=i: bv("XN", 2048 * c, 2048 * (c + 1))
                           .rearrange("p (k m) -> p k m", m=512)[:, :, 128 * i:128 * (i + 1)],
                           lambda c, i=i: bv("XN", 8192 + 2048 * c, 8192 + 2048 * (c + 1))
                           .rearrange("p (k m) -> p k m", m=512)[:, :, 128 * i:128 * (i + 1)],
                           xnown, xnown)
        p.barrier("P1b_%d" % g)
        if stop_here("P1b_%d" % g):
            return nc

        for slab in range(8):
            F1BANKS = [0, 1, 6, 7]

            def w1_load(fg):
                load(bv("AB", A_W1 + 2048 * (fg % 3), A_W1 + 2048 * (fg % 3) + 2048),
                     w1B[fg, :, :], w1buf[fg % 3])
            if slab == 0:
                if g == 0:
                    p.raw_wait("sync", "GXA", p.cnt.get("GXA", 0))
                w1_load(0)
                w1_load(1)
            for fc in range(8):
                fg = slab * 8 + fc
                wb = w1buf[fg % 3]
                if fg + 2 < 64:
                    if g == 0 and fg + 2 == 32:
                        p.raw_wait("sync", "GXB", p.cnt.get("GXB", 0))
                    w1_load(fg + 2)
                bk = F1BANKS[fc % 4]
                rb = BF("rnd%d" % (fc % 4))
                RQH = ("FP1", 0) if fc % 4 == 0 else ("FP1", 2048) if fc % 4 == 1 else \
                      ("FPC", 0) if fc % 4 == 2 else ("FPC", 2048)
                for kc in range(16):
                    p.do("tensor", lambda e, kc=kc, fg=fg, bk=bk: e.matmul(
                        banks[bk][:, 0:512],
                        bv("AB", A_W1 + 2048 * (fg % 3) + 128 * kc, A_W1 + 2048 * (fg % 3) + 128 * (kc + 1)),
                        bv("XN", 512 * kc, 512 * (kc + 1)),
                        start=(kc == 0), stop=False),
                        reads=[wb, xnown], writes=[pbank[bk]], accum=(kc != 0))
                    p.do("tensor", lambda e, kc=kc, fg=fg, bk=bk: e.matmul(
                        banks[bk][:, 0:512],
                        bv("AB", A_W1 + 2048 * (fg % 3) + 128 * kc, A_W1 + 2048 * (fg % 3) + 128 * (kc + 1)),
                        bv("XN", 8192 + 512 * kc, 8192 + 512 * (kc + 1)),
                        start=False, stop=(kc == 15)),
                        reads=[wb, xnown], writes=[pbank[bk]], accum=True)
                p.do("scalar", lambda e, bk=bk, RQH=RQH: e.activation(
                    hv(RQH[0], RQH[1] + F_Z, RQH[1] + F_Z + 512), banks[bk][:, 0:512], AF.Copy, scale=10.0),
                    reads=[pbank[bk]], writes=[rb])
                it = int_t
                itb = BF("int_t0")
                p.do("vector", lambda e, RQH=RQH, it=it: e.tensor_copy(
                    it[:, :], hv(RQH[0], RQH[1] + F_Z, RQH[1] + F_Z + 512)),
                    reads=[rb, itb], writes=[itb])
                p.do("vector", lambda e, RQH=RQH, it=it: e.tensor_copy(
                    hv(RQH[0], RQH[1] + F_KF, RQH[1] + F_KF + 512), it[:, :]),
                    reads=[itb], writes=[rb])
                p.do("vector", lambda e, RQH=RQH: e.scalar_tensor_tensor(
                    hv(RQH[0], RQH[1] + F_GH, RQH[1] + F_GH + 512), hv(RQH[0], RQH[1] + F_Z, RQH[1] + F_Z + 512), 0.5,
                    hv(RQH[0], RQH[1] + F_KF, RQH[1] + F_KF + 512), ALU.subtract, ALU.is_gt),
                    reads=[rb], writes=[rb])
                p.do("vector", lambda e, RQH=RQH: e.scalar_tensor_tensor(
                    hv(RQH[0], RQH[1] + F_GL, RQH[1] + F_GL + 512), hv(RQH[0], RQH[1] + F_Z, RQH[1] + F_Z + 512), 0.5,
                    hv(RQH[0], RQH[1] + F_KF, RQH[1] + F_KF + 512), ALU.add, ALU.is_lt),
                    reads=[rb], writes=[rb])
                p.do("vector", lambda e, RQH=RQH: e.tensor_sub(
                    hv(RQH[0], RQH[1] + F_GH, RQH[1] + F_GH + 512), hv(RQH[0], RQH[1] + F_GH, RQH[1] + F_GH + 512),
                    hv(RQH[0], RQH[1] + F_GL, RQH[1] + F_GL + 512)),
                    reads=[rb], writes=[rb])
                p.do("vector", lambda e, RQH=RQH: e.tensor_add(
                    hv(RQH[0], RQH[1] + F_KF, RQH[1] + F_KF + 512), hv(RQH[0], RQH[1] + F_KF, RQH[1] + F_KF + 512),
                    hv(RQH[0], RQH[1] + F_GH, RQH[1] + F_GH + 512)),
                    reads=[rb], writes=[rb])
                p.do("vector", lambda e, RQH=RQH: e.tensor_scalar(
                    hv(RQH[0], RQH[1] + F_KF, RQH[1] + F_KF + 512), hv(RQH[0], RQH[1] + F_KF, RQH[1] + F_KF + 512),
                    QMAX, QMIN, ALU.min, ALU.max),
                    reads=[rb], writes=[rb])
                p.do("scalar", lambda e, RQH=RQH: e.activation(
                    hv(RQH[0], RQH[1] + F_Z, RQH[1] + F_Z + 512), hv(RQH[0], RQH[1] + F_KF, RQH[1] + F_KF + 512),
                    AF.Gelu_apprx_tanh, scale=0.1),
                    reads=[rb], writes=[rb])
                p.do("scalar", lambda e, fc=fc, RQH=RQH: e.activation(
                    bv("AB", A_AP + 512 * fc, A_AP + 512 * (fc + 1)),
                    hv(RQH[0], RQH[1] + F_Z, RQH[1] + F_Z + 512), AF.Copy),
                    reads=[rb], writes=[aslab_h[fc]])
                p.do("gpsimd", lambda e, fc=fc, RQH=RQH: e.tensor_tensor(
                    bv("AB", A_AP + 4096 + 512 * fc, A_AP + 4096 + 512 * (fc + 1)),
                    hv(RQH[0], RQH[1] + F_Z, RQH[1] + F_Z + 512),
                    bv("AB", A_AP + 512 * fc, A_AP + 512 * (fc + 1)), ALU.subtract),
                    reads=[rb, aslab_h[fc]], writes=[aslab_l[fc]])
                if g == 0 and deferred_splits:
                    sw = deferred_splits.pop(0)
                    split_weight(sw[0], sw[1], sw[2], sw[3], late=True)
            for nt in range(4):
                def w2_load(kc, nt=nt):
                    fg2 = slab * 8 + kc
                    load(bv("AB", A_W2 + 512 * (kc % 3), A_W2 + 512 * (kc % 3) + 512),
                         w2B[128 * fg2:128 * (fg2 + 1), 512 * nt:512 * (nt + 1)], w2buf[kc % 3])
                if g == 0 and slab == 0 and nt == 0:
                    p.raw_wait("sync", "GXC", p.cnt.get("GXC", 0))
                if g == 0 and slab == 4 and nt == 0:
                    p.raw_wait("sync", "GXD", p.cnt.get("GXD", 0))
                w2_load(0)
                w2_load(1)
                for kc in range(8):
                    wb = w2buf[kc % 3]
                    if kc + 2 < 8:
                        w2_load(kc + 2)
                    for tt in range(4):
                        p.do("tensor", lambda e, kc=kc, tt=tt: e.matmul(
                            banks[2 + tt][:, 0:512],
                            bv("AB", A_AP + 512 * kc + 128 * tt, A_AP + 512 * kc + 128 * (tt + 1)),
                            bv("AB", A_W2 + 512 * (kc % 3), A_W2 + 512 * (kc % 3) + 512),
                            start=(kc == 0), stop=False),
                            reads=[aslab_h[kc], wb], writes=[pbank[2 + tt]], accum=(kc != 0))
                        p.do("tensor", lambda e, kc=kc, tt=tt: e.matmul(
                            banks[2 + tt][:, 0:512],
                            bv("AB", A_AP + 4096 + 512 * kc + 128 * tt, A_AP + 4096 + 512 * kc + 128 * (tt + 1)),
                            bv("AB", A_W2 + 512 * (kc % 3), A_W2 + 512 * (kc % 3) + 512),
                            start=False, stop=(kc == 7)),
                            reads=[aslab_l[kc], wb], writes=[pbank[2 + tt]], accum=True)
                if g == 0 and deferred_splits:
                    sw = deferred_splits.pop(0)
                    split_weight(sw[0], sw[1], sw[2], sw[3], late=True)
                for tt in range(4):
                    if slab == 0:
                        p.do("vector", lambda e, tt=tt, nt=nt: e.tensor_copy(
                            hv("FPG", 2048 * tt + 512 * nt, 2048 * tt + 512 * (nt + 1)),
                            banks[2 + tt][:, 0:512]),
                            reads=[pbank[2 + tt]], writes=[gb])
                    else:
                        p.do("vector", lambda e, tt=tt, nt=nt: e.tensor_add(
                            hv("FPG", 2048 * tt + 512 * nt, 2048 * tt + 512 * (nt + 1)),
                            hv("FPG", 2048 * tt + 512 * nt, 2048 * tt + 512 * (nt + 1)),
                            banks[2 + tt][:, 0:512]),
                            reads=[pbank[2 + tt], gb], writes=[gb])
            if g == 0:
                for _ in range(2):
                    if gx_p1end:
                        sx, dx, semx = gx_p1end.pop(0)
                        cast_gx(sx, dx, semx)
                for _ in range(2 if slab < 5 else 3):
                    if gx_p2a:
                        sx, dx, semx = gx_p2a.pop(0)
                        cast_gx(sx, dx, semx)
                    elif gx_p2b:
                        sx, dx, semx = gx_p2b.pop(0)
                        cast_gx(sx, dx, semx)
        p.barrier("P2_%d" % g)
        if stop_here("P2_%d" % g):
            return nc

        def cq_load(seq):
            kcl = seq % 16
            sl3 = seq % 3
            _s = next_slot()
            load(bv("AB", A_CQ + 3072 * sl3, A_CQ + 3072 * sl3 + 1536),
                 cqB[0, :, 1536 * kcl:1536 * (kcl + 1)], cqbuf[sl3], slot=_s)
            load(bv("AB", A_CQ + 3072 * sl3 + 1536, A_CQ + 3072 * sl3 + 3072),
                 cqB[1, :, 1536 * kcl:1536 * (kcl + 1)], cqbuf[sl3], slot=_s)

        if g == 0:
            p.raw_wait("sync", "GXE", p.cnt.get("GXE", 0))
        cq_load(0)
        cq_load(1)
        for hm in range(2):
            for kc in range(16):
                seq = hm * 16 + kc
                sl3 = seq % 3
                wb = cqbuf[sl3]
                if seq + 2 < 32:
                    cq_load(seq + 2)
                for m6 in range(6):
                    mc = hm * 6 + m6
                    mm3w(banks[m6][:, 0:512], None,
                         bv("AB", A_CQ + 3072 * sl3 + 128 * mc, A_CQ + 3072 * sl3 + 128 * (mc + 1)),
                         bv("AB", A_CQ + 3072 * sl3 + 1536 + 128 * mc, A_CQ + 3072 * sl3 + 1536 + 128 * (mc + 1)),
                         bv("XN", 512 * kc, 512 * (kc + 1)),
                         bv("XN", 8192 + 512 * kc, 8192 + 512 * (kc + 1)),
                         [wb], [xnown], kc == 0, kc == 15, n=m6)
            for m6 in range(6):
                mc = hm * 6 + m6
                evac_pair(banks[m6][:, 0:512], m6,
                          bv("NQ", 512 * mc, 512 * (mc + 1)),
                          bv("NQ", 6144 + 512 * mc, 6144 + 512 * (mc + 1)), nqb, nqb,
                          scale=hv("consts", C_GQ + mc, C_GQ + mc + 1))
                p.do("scalar", lambda e, m6=m6: e.activation(
                    hv("FP1", F_SQ, F_SQ + 512), banks[m6][:, 0:512], AF.Square),
                    reads=[pbank[m6], nqb], writes=[rqrow])
                p.do("tensor", lambda e, mc=mc: e.matmul(
                    banks[6][0:1, 0:512], hv("consts", C_ONES, C_ONES + 1),
                    hv("FP1", F_SQ, F_SQ + 512),
                    start=(mc == 0), stop=(mc == 11)),
                    reads=[rqrow, cstb], writes=[pbank[6]], accum=(mc != 0))
        p.do("scalar", lambda e: e.activation(hs("FP1", 64, 65, F_SQ, F_SQ + 512),
                                              banks[6][0:1, 0:512],
                                              AF.Sqrt, bias=hs("consts", 64, 65, C_EPS, C_EPS + 1),
                                              scale=1.0 / R_Q),
             reads=[pbank[6], cstb], writes=[rqrow])
        p.do("vector", lambda e: e.reciprocal(hs("FP1", 64, 65, F_SQ, F_SQ + 512),
                                              hs("FP1", 64, 65, F_SQ, F_SQ + 512)),
             reads=[rqrow], writes=[rqrow])
        for i in range(4):
            p.do("tensor", lambda e, i=i: e.matmul(
                banks[7][:, 0:1], hs("FP1", 64, 65, F_SQ + 128 * i, F_SQ + 128 * (i + 1)),
                hs("consts", 64, 65, C_ONES, C_ONES + 1), start=True, stop=True),
                reads=[rqrow, cstb], writes=[pbank[7]])
            p.do("vector", lambda e, i=i: e.tensor_scalar_mul(
                hv("stats", ST_SCQ + i, ST_SCQ + i + 1), banks[7][:, 0:1], ATTN_SCALE),
                reads=[pbank[7]], writes=[scqb])
        load(hv("FPC", C_MASK, C_MASK + 2048).rearrange("p (i m) -> p i m", m=512),
             lm_d[4 * g:4 * (g + 1), :, :].rearrange("i p m -> p i m"), maskb)

        def load_kv(h):
            par = h % 2
            kv = kvb[par]
            _s = next_slot()
            load(bv("AB", A_K + 4096 * par, A_K + 4096 * par + 2048), kkB[0, h], kv, slot=_s)
            load(bv("AB", A_K + 4096 * par + 2048, A_K + 4096 * par + 4096), kkB[1, h], kv, slot=_s)
            hg, hi4 = h // 4, h % 4
            src0 = vvB[0, hg][:, :].rearrange("p (kt m) -> p kt m", m=512)[:, :, 128 * hi4:128 * (hi4 + 1)]
            src1 = vvB[1, hg][:, :].rearrange("p (kt m) -> p kt m", m=512)[:, :, 128 * hi4:128 * (hi4 + 1)]
            load(bv("AB", A_V + 4096 * par, A_V + 4096 * par + 2048)
                 .rearrange("p (kt m) -> p kt m", m=128), src0, kv, slot=_s)
            load(bv("AB", A_V + 4096 * par + 2048, A_V + 4096 * par + 4096)
                 .rearrange("p (kt m) -> p kt m", m=128), src1, kv, slot=_s)
            load(bs("AB", 0, 64, A_KROT + 4096 * par, A_KROT + 4096 * par + 2048), krotB[0, h], kv, slot=_s)
            load(bs("AB", 0, 64, A_KROT + 4096 * par + 2048, A_KROT + 4096 * par + 4096), krotB[1, h], kv, slot=_s)

        def load_dq(h):
            _s = next_slot()
            load(bv("AB", A_DQN, A_DQN + 1536), dqnB[0, h], dqp, slot=_s)
            load(bv("AB", A_DQN + 1536, A_DQN + 3072), dqnB[1, h], dqp, slot=_s)
            load(bv("AB", A_DQR, A_DQR + 768), dqrB[0, h], dqp, slot=_s)
            load(bv("AB", A_DQR + 768, A_DQR + 1536), dqrB[1, h], dqp, slot=_s)

        if g == 0:
            p.raw_wait("sync", "GXE", p.cnt.get("GXE", 0))
        load_dq(0)
        p.barrier("P3_%d" % g)
        if stop_here("P3_%d" % g):
            return nc

        load_kv(0)
        for h in range(H):
            par = h % 2
            kv = kvb[par]
            cos_h = hs("consts", 0, 64, C_COS + h, C_COS + h + 1)
            spm_h = hs("consts", 0, 64, C_SPM + h, C_SPM + h + 1)
            if h + 1 < H:
                load_kv(h + 1)
            for kc in range(12):
                mm3w(banks[4][:, 0:512], None,
                     bv("AB", A_DQN + 128 * kc, A_DQN + 128 * (kc + 1)),
                     bv("AB", A_DQN + 1536 + 128 * kc, A_DQN + 1536 + 128 * (kc + 1)),
                     bv("NQ", 512 * kc, 512 * (kc + 1)),
                     bv("NQ", 6144 + 512 * kc, 6144 + 512 * (kc + 1)),
                     [dqp], [nqb], kc == 0, kc == 11, n=4)
            evac_pair(banks[4][:, 0:512], 4,
                      bv("AB", A_QN, A_QN + 512), bv("AB", A_QN + 512, A_QN + 1024), qnb, qnb)
            for kc in range(12):
                mm3w(banks[5][0:64, 0:512], None,
                     bv("AB", A_DQR + 64 * kc, A_DQR + 64 * (kc + 1)),
                     bv("AB", A_DQR + 768 + 64 * kc, A_DQR + 768 + 64 * (kc + 1)),
                     bv("NQ", 512 * kc, 512 * (kc + 1)),
                     bv("NQ", 6144 + 512 * kc, 6144 + 512 * (kc + 1)),
                     [dqp], [nqb], kc == 0, kc == 11, n=5)
            if h + 1 < H:
                load_dq(h + 1)
            p.do("vector", lambda e: e.tensor_copy(hs("FP1", 0, 64, F_QRF, F_QRF + 512),
                                                   banks[5][0:64, 0:512]),
                 reads=[pbank[5]], writes=[qrf])
            p.do("tensor", lambda e: e.matmul(
                banks[6][0:64, 0:512], swap64, hs("FP1", 0, 64, F_QRF, F_QRF + 512),
                start=True, stop=True),
                reads=[qrf, cstb], writes=[pbank[6]])
            p.do("vector", lambda e: e.tensor_copy(hs("FP1", 0, 64, F_QSF, F_QSF + 512),
                                                   banks[6][0:64, 0:512]),
                 reads=[pbank[6]], writes=[qrf])
            p.do("vector", lambda e, cos_h=cos_h: e.tensor_scalar_mul(
                hs("FP1", 0, 64, F_QOF, F_QOF + 512), hs("FP1", 0, 64, F_QRF, F_QRF + 512), cos_h),
                reads=[qrf, cstb], writes=[qrotb])
            p.do("vector", lambda e, spm_h=spm_h: e.scalar_tensor_tensor(
                hs("FP1", 0, 64, F_QOF, F_QOF + 512), hs("FP1", 0, 64, F_QSF, F_QSF + 512), spm_h,
                hs("FP1", 0, 64, F_QOF, F_QOF + 512), ALU.mult, ALU.add),
                reads=[qrf, cstb, qrotb], writes=[qrotb])
            p.do("scalar", lambda e: e.activation(
                bs("AB", 0, 64, A_QROT, A_QROT + 512), hs("FP1", 0, 64, F_QOF, F_QOF + 512), AF.Copy),
                reads=[qrotb], writes=[qrotb])
            p.do("vector", lambda e: e.tensor_tensor(
                bs("AB", 0, 64, A_QROT + 512, A_QROT + 1024), hs("FP1", 0, 64, F_QOF, F_QOF + 512),
                bs("AB", 0, 64, A_QROT, A_QROT + 512), ALU.subtract),
                reads=[qrotb], writes=[qrotb])
            def emit_scores(i):
                s_i = svals[i]
                for st in range(s_i):
                    bk = 4 + st
                    qn_h = bv("AB", A_QN + 128 * i, A_QN + 128 * (i + 1))
                    qn_l = bv("AB", A_QN + 512 + 128 * i, A_QN + 512 + 128 * (i + 1))
                    kn_h = bv("AB", A_K + 4096 * par + 512 * st, A_K + 4096 * par + 512 * (st + 1))
                    kn_l = bv("AB", A_K + 4096 * par + 2048 + 512 * st, A_K + 4096 * par + 2048 + 512 * (st + 1))
                    qr_h = bs("AB", 0, 64, A_QROT + 128 * i, A_QROT + 128 * (i + 1))
                    qr_l = bs("AB", 0, 64, A_QROT + 512 + 128 * i, A_QROT + 512 + 128 * (i + 1))
                    kr_h = bs("AB", 0, 64, A_KROT + 4096 * par + 512 * st, A_KROT + 4096 * par + 512 * (st + 1))
                    kr_l = bs("AB", 0, 64, A_KROT + 4096 * par + 2048 + 512 * st, A_KROT + 4096 * par + 2048 + 512 * (st + 1))
                    p.do("tensor", lambda e, o=banks[bk][:, 0:512], a=qn_h, b=kn_h: e.matmul(
                        o, a, b, start=True, stop=False),
                        reads=[qnb, kv], writes=[pbank[bk]])
                    p.do("tensor", lambda e, o=banks[bk][:, 0:512], a=qn_h, b=kn_l: e.matmul(
                        o, a, b, start=False, stop=False),
                        reads=[qnb, kv], writes=[pbank[bk]], accum=True)
                    p.do("tensor", lambda e, o=banks[bk][:, 0:512], a=qn_l, b=kn_h: e.matmul(
                        o, a, b, start=False, stop=False),
                        reads=[qnb, kv], writes=[pbank[bk]], accum=True)
                    p.do("tensor", lambda e, o=banks[bk][:, 0:512], a=qr_h, b=kr_h: e.matmul(
                        o, a, b, start=False, stop=False),
                        reads=[qrotb, kv], writes=[pbank[bk]], accum=True)
                    p.do("tensor", lambda e, o=banks[bk][:, 0:512], a=qr_h, b=kr_l: e.matmul(
                        o, a, b, start=False, stop=False),
                        reads=[qrotb, kv], writes=[pbank[bk]], accum=True)
                    p.do("tensor", lambda e, o=banks[bk][:, 0:512], a=qr_l, b=kr_h: e.matmul(
                        o, a, b, start=False, stop=True),
                        reads=[qrotb, kv], writes=[pbank[bk]], accum=True)
                bklast = 4 + s_i - 1
                p.do("vector", lambda e, i=i, bklast=bklast: e.tensor_add(
                    banks[bklast][:, 0:512], banks[bklast][:, 0:512],
                    hv("FPC", C_MASK + 512 * i, C_MASK + 512 * (i + 1))),
                    reads=[pbank[bklast], maskb], writes=[pbank[bklast]])

            def emit_softmax(i):
                s_i = svals[i]
                for st in range(s_i):
                    p.do("scalar", lambda e, i=i, st=st: e.activation(
                        hv("FP1", F_PROBS + 512 * st, F_PROBS + 512 * (st + 1)),
                        banks[4 + st][:, 0:512], AF.Exp,
                        bias=hv("consts", C_M30, C_M30 + 1),
                        scale=hv("stats", ST_SCQ + i, ST_SCQ + i + 1),
                        accum_out=hv("stats", ST_DEN + st, ST_DEN + st + 1)),
                        reads=[pbank[4 + st], cstb, scqb], writes=[probs, statb])
                p.do("vector", lambda e, s_i=s_i: e.reduce_sum(
                    hv("stats", ST_RC, ST_RC + 1), hv("stats", ST_DEN, ST_DEN + s_i), axis=AX.X),
                    reads=[statb], writes=[statb])
                p.do("vector", lambda e: e.reciprocal(
                    hv("stats", ST_RC, ST_RC + 1), hv("stats", ST_RC, ST_RC + 1)),
                    reads=[statb], writes=[statb])
                p.do("vector", lambda e, s_i=s_i: e.tensor_scalar_mul(
                    hv("FP1", F_PROBS, F_PROBS + 512 * s_i), hv("FP1", F_PROBS, F_PROBS + 512 * s_i),
                    hv("stats", ST_RC, ST_RC + 1)),
                    reads=[probs, statb], writes=[probs])

            def emit_pv(i):
                s_i = svals[i]
                nkc = 4 * s_i

                def pv_transpose(kc):
                    bt = 1 + kc % 3
                    p.do("tensor", lambda e, kc=kc, bt=bt: e.transpose(
                        banks[bt][:, 0:128], hv("FP1", F_PROBS + 128 * kc, F_PROBS + 128 * (kc + 1)),
                        ident),
                        reads=[probs, cstb], writes=[pbank[bt]])
                    evac_pair(banks[bt][:, 0:128], bt,
                              bv("AB", A_PT + 256 * (kc % 3), A_PT + 256 * (kc % 3) + 128),
                              bv("AB", A_PT + 256 * (kc % 3) + 128, A_PT + 256 * (kc % 3) + 256),
                              ptb[kc % 3], ptb[kc % 3])

                for kc in range(min(2, nkc)):
                    pv_transpose(kc)
                for kc in range(nkc):
                    if kc + 2 < nkc:
                        pv_transpose(kc + 2)
                    v_h = bv("AB", A_V + 4096 * par + 128 * kc, A_V + 4096 * par + 128 * (kc + 1))
                    v_l = bv("AB", A_V + 4096 * par + 2048 + 128 * kc, A_V + 4096 * par + 2048 + 128 * (kc + 1))
                    t_h = bv("AB", A_PT + 256 * (kc % 3), A_PT + 256 * (kc % 3) + 128)
                    t_l = bv("AB", A_PT + 256 * (kc % 3) + 128, A_PT + 256 * (kc % 3) + 256)
                    p.do("tensor", lambda e, o=banks[0][:, 0:128], a=v_h, b=t_h, kc=kc: e.matmul(
                        o, a, b, start=(kc == 0), stop=False),
                        reads=[kv, ptb[kc % 3]], writes=[pbank[0]], accum=(kc != 0))
                    p.do("tensor", lambda e, o=banks[0][:, 0:128], a=v_h, b=t_l: e.matmul(
                        o, a, b, start=False, stop=False),
                        reads=[kv, ptb[kc % 3]], writes=[pbank[0]], accum=True)
                    p.do("tensor", lambda e, o=banks[0][:, 0:128], a=v_l, b=t_h, kc=kc, nkc=nkc: e.matmul(
                        o, a, b, start=False, stop=(kc == nkc - 1)),
                        reads=[kv, ptb[kc % 3]], writes=[pbank[0]], accum=True)
                evac_pair(banks[0][:, 0:128], 0,
                          bv("XN", 512 * h + 128 * i, 512 * h + 128 * (i + 1)),
                          bv("XN", 8192 + 512 * h + 128 * i, 8192 + 512 * h + 128 * (i + 1)),
                          outfm, outfm)

            emit_scores(0)
            for i in range(4):
                emit_softmax(i)
                if i + 1 < 4:
                    emit_scores(i + 1)
                emit_pv(i)
        if g == 0:
            p.raw_wait("sync", "GXF", p.cnt.get("GXF", 0))
        _s = next_slot()
        load(bv("AB", A_PR, A_PR + 2048), prB[0, :, 0:2048], prbuf[0], slot=_s,
             extra_writes=(kvb[0],))
        load(bv("AB", A_PR + 2048, A_PR + 4096), prB[1, :, 0:2048], prbuf[0], slot=_s,
             extra_writes=(kvb[0],))
        _s = next_slot()
        load(bv("AB", A_PR + 4096, A_PR + 6144), prB[0, :, 2048:4096], prbuf[1], slot=_s,
             extra_writes=(kvb[1],))
        load(bv("AB", A_PR + 6144, A_PR + 8192), prB[1, :, 2048:4096], prbuf[1], slot=_s,
             extra_writes=(kvb[1],))
        p.barrier("P4end_%d" % g)
        if stop_here("P4_%d" % g):
            return nc

        def pr_load(seq):
            kcl = seq % 16
            sl3 = seq % 3
            _s = next_slot()
            load(bv("AB", A_PR + 4096 * sl3, A_PR + 4096 * sl3 + 2048),
                 prB[0, :, 2048 * kcl:2048 * (kcl + 1)], prbuf[sl3], slot=_s)
            load(bv("AB", A_PR + 4096 * sl3 + 2048, A_PR + 4096 * sl3 + 4096),
                 prB[1, :, 2048 * kcl:2048 * (kcl + 1)], prbuf[sl3], slot=_s)


        for tp in range(2):
            for kc in range(16):
                seq = tp * 16 + kc
                sl3 = seq % 3
                wb = prbuf[sl3]
                if seq + 2 < 32:
                    pr_load(seq + 2)
                for tt2 in range(2):
                    for nt in range(4):
                        o_h = bv("XN", 512 * kc + 128 * (tp * 2 + tt2), 512 * kc + 128 * (tp * 2 + tt2) + 128)
                        o_l = bv("XN", 8192 + 512 * kc + 128 * (tp * 2 + tt2), 8192 + 512 * kc + 128 * (tp * 2 + tt2) + 128)
                        p_h = bv("AB", A_PR + 4096 * sl3 + 512 * nt, A_PR + 4096 * sl3 + 512 * (nt + 1))
                        p_l = bv("AB", A_PR + 4096 * sl3 + 2048 + 512 * nt, A_PR + 4096 * sl3 + 2048 + 512 * (nt + 1))
                        nb = tt2 * 4 + nt
                        p.do("tensor", lambda e, o=banks[nb][:, 0:512], a=o_h, b=p_h, kc=kc: e.matmul(
                            o, a, b, start=(kc == 0), stop=False),
                            reads=[outfm, wb], writes=[pbank[nb]], accum=(kc != 0))
                        p.do("tensor", lambda e, o=banks[nb][:, 0:512], a=o_l, b=p_h: e.matmul(
                            o, a, b, start=False, stop=False),
                            reads=[outfm, wb], writes=[pbank[nb]], accum=True)
                        p.do("tensor", lambda e, o=banks[nb][:, 0:512], a=o_h, b=p_l, kc=kc: e.matmul(
                            o, a, b, start=False, stop=(kc == 15)),
                            reads=[outfm, wb], writes=[pbank[nb]], accum=True)
            for tt2 in range(2):
                tt = tp * 2 + tt2
                sl = g * 4 + tt
                load(hv("FPC", C_XTOK, C_XTOK + 2048), xq_d[128 * sl:128 * (sl + 1), :], xtok)
                for nt in range(4):
                    nb = tt2 * 4 + nt
                    p.do("scalar", lambda e, tt=tt, nt=nt: e.activation(
                        hv("FP1", F_PROBS, F_PROBS + 512),
                        hv("FPG", 2048 * tt + 512 * nt, 2048 * tt + 512 * (nt + 1)), AF.Silu),
                        reads=[gb], writes=[sgb])
                    p.do("vector", lambda e, nb=nb: e.tensor_mul(
                        hv("FP1", F_PROBS + 512, F_PROBS + 1024), banks[nb][:, 0:512],
                        hv("FP1", F_PROBS, F_PROBS + 512)),
                        reads=[pbank[nb], sgb], writes=[sgb])
                    p.do("vector", lambda e, nt=nt: e.tensor_add(
                        hv("FP1", F_PROBS + 1024, F_PROBS + 1536),
                        hv("FP1", F_PROBS + 512, F_PROBS + 1024),
                        hv("FPC", C_XTOK + 512 * nt, C_XTOK + 512 * (nt + 1))),
                        reads=[sgb, xtok], writes=[obuf])
                    store(out_d[128 * sl:128 * (sl + 1), 512 * nt:512 * (nt + 1)],
                          hv("FP1", F_PROBS + 1024, F_PROBS + 1536), obuf)
        p.barrier("P5_%d" % g)

    for sn in list(p.cnt):
        if sn.startswith("ST"):
            p.raw_wait("gpsimd", sn, p.cnt.get(sn, 0))
        if sn.startswith("SY"):
            p.raw_wait("sync", sn, p.cnt.get(sn, 0))
    with nc.Block() as block:
        p.emit(block)
    return nc


def _stage_inputs(L0_attn_x, L0_attn_freqs_cos, L0_attn_freqs_sin, L0_attn_w_cq,
                  L0_attn_q_norm_w, L0_attn_w_dq_nope, L0_attn_w_dq_rope, L0_attn_w_ckv,
                  L0_attn_kv_norm_w, L0_attn_w_dk_nope, L0_attn_w_dv, L0_attn_w_k_rope,
                  L0_attn_w_proj, L0_ffn_W1, L0_ffn_W2):
    f32 = np.float32
    x = np.asarray(L0_attn_x, f32)

    def rope_perm(w, nh):
        w = np.asarray(w, f32)
        d = w.shape[1] // nh
        cols = []
        for h in range(nh):
            base = h * d
            cols += [base + 2 * j for j in range(d // 2)]
            cols += [base + 2 * j + 1 for j in range(d // 2)]
        return np.ascontiguousarray(w[:, cols])
    w_dqr = rope_perm(L0_attn_w_dq_rope, H)
    w_kr = rope_perm(L0_attn_w_k_rope, 1)

    consts = np.zeros((128, 576), f32)
    consts[:, 0:128] = np.eye(128, dtype=f32)
    consts[:, 128:256] = 1.0
    sw = np.zeros((128, 128), f32)
    for i in range(64):
        sw[(i + 32) % 64, i] = 1.0
    consts[:, 256:384] = sw
    consts[:, 384:396] = np.asarray(L0_attn_q_norm_w, f32).reshape(12, 128).T
    consts[:, 396:400] = np.asarray(L0_attn_kv_norm_w, f32).reshape(4, 128).T
    cos = np.asarray(L0_attn_freqs_cos, f32)
    sin = np.asarray(L0_attn_freqs_sin, f32)
    for h in range(16):
        consts[0:32, 400 + h] = cos[h]
        consts[32:64, 400 + h] = cos[h]
        consts[0:32, 416 + h] = sin[h]
        consts[32:64, 416 + h] = sin[h]
    consts[0:32, 432] = -1.0
    consts[32:64, 432] = 1.0
    consts[:, 449] = 1e-5
    consts[:, 450] = -30.0

    def chunk_heads(w, nh, dm):
        K = w.shape[0]
        nkc = K // 128
        a = w.reshape(nkc, 128, nh, dm).transpose(2, 1, 0, 3).reshape(nh, 128, nkc * dm)
        return np.ascontiguousarray(a)

    w_dqn = np.asarray(L0_attn_w_dq_nope, f32)
    w_dkn = np.asarray(L0_attn_w_dk_nope, f32)
    w_dv = np.asarray(L0_attn_w_dv, f32)
    w_ckv = np.asarray(L0_attn_w_ckv, f32)
    W1 = np.asarray(L0_ffn_W1, f32)

    dv4 = w_dv.reshape(4, 128, 4, 4, 128)
    dv4 = dv4.transpose(2, 1, 0, 3, 4).reshape(4, 128, 4 * 512)

    shared = {
        "consts": consts,
        "wcqs": np.ascontiguousarray(np.asarray(L0_attn_w_cq, f32)
                                     .reshape(16, 128, 1536).transpose(1, 0, 2)
                                     .reshape(128, 16 * 1536)),
        "wdqns": chunk_heads(w_dqn, H, 128),
        "wdqrs": chunk_heads(w_dqr, H, 64),
        "wckvs": np.ascontiguousarray(
            w_ckv.reshape(16, 128, 512).transpose(1, 0, 2).reshape(128, 16 * 512)),
        "wdkns": chunk_heads(w_dkn, H, 128),
        "wdvs4": np.ascontiguousarray(dv4),
        "wkrs": np.ascontiguousarray(
            w_kr.reshape(16, 128, 64).transpose(1, 0, 2).reshape(128, 16 * 64)),
        "wprs": np.ascontiguousarray(np.asarray(L0_attn_w_proj, f32)
                                     .reshape(16, 128, 2048).transpose(1, 0, 2)
                                     .reshape(128, 16 * 2048)),
        "w1s": np.ascontiguousarray(
            W1.T.reshape(16, 128, 64, 128).transpose(2, 1, 0, 3).reshape(64, 128, 16 * 128)),
        "w2t": np.ascontiguousarray(np.asarray(L0_ffn_W2, f32).T),
    }
    in_maps = []
    for c in range(8):
        b, half = c // 2, c % 2
        slots = OWN[half]
        xq = np.ascontiguousarray(
            np.concatenate([x[b, 128 * gt:128 * (gt + 1), :] for gt in slots], axis=0))
        lm = np.zeros((8, 128, 512), f32)
        for i, gt in enumerate(slots):
            s_i = SLIST[i]
            keys = (s_i - 1) * 512 + np.arange(512)
            qpos = gt * 128 + np.arange(128)
            lm[i] = np.where(keys[None, :] <= qpos[:, None], 0.0, NEG)
        m = dict(shared)
        m["xq"] = xq
        m["xnat"] = np.ascontiguousarray(x[b])
        m["lastmask"] = lm
        in_maps.append(m)
    return in_maps


def kernel(L0_attn_x, L0_attn_mask, L0_attn_freqs_cos, L0_attn_freqs_sin, L0_attn_w_cq,
           L0_attn_q_norm_w, L0_attn_w_dq_nope, L0_attn_w_dq_rope, L0_attn_w_ckv,
           L0_attn_kv_norm_w, L0_attn_w_dk_nope, L0_attn_w_dv, L0_attn_w_k_rope,
           L0_attn_w_proj, L0_ffn_W1, L0_ffn_W2):
    in_maps = _stage_inputs(L0_attn_x, L0_attn_freqs_cos, L0_attn_freqs_sin, L0_attn_w_cq,
                            L0_attn_q_norm_w, L0_attn_w_dq_nope, L0_attn_w_dq_rope,
                            L0_attn_w_ckv, L0_attn_kv_norm_w, L0_attn_w_dk_nope,
                            L0_attn_w_dv, L0_attn_w_k_rope, L0_attn_w_proj,
                            L0_ffn_W1, L0_ffn_W2)
    nc = build_program()
    res = run_bass_kernel_spmd(nc, in_maps, core_ids=list(range(8)))
    out = np.empty((B, S, D), np.float32)
    for c in range(8):
        b, half = c // 2, c % 2
        r = np.asarray(res.results[c]["out"])
        for i, gt in enumerate(OWN[half]):
            out[b, 128 * gt:128 * (gt + 1), :] = r[128 * i:128 * (i + 1), :]
    return out


# version 28
# speedup vs baseline: 1.0445x; 1.0445x over previous
import math
import numpy as np

import concourse.bass as bass
import concourse.mybir as mybir
from concourse.bass_utils import run_bass_kernel_spmd

F32 = mybir.dt.float32
BF16 = mybir.dt.bfloat16
I32 = mybir.dt.int32
AF = mybir.ActivationFunctionType
ALU = mybir.AluOpType
AX = mybir.AxisListType

B, S, D = 4, 2048, 2048
H, D_NOPE, D_ROPE, D_V = 16, 128, 64, 128
R_Q, R_KV = 1536, 512
D_FF = 8192
EPS = 1e-5
ATTN_SCALE = float(np.float32(1.0) / np.sqrt(np.float32(D_NOPE + D_ROPE)))
V_SCALE = float(np.float32(1.0) / np.sqrt(np.float32(H * D_V)))
QMAX, QMIN = 32767.0, -32768.0
NEG = -1.0e30

NT = 16
OWN = [[0, 3, 4, 7, 8, 11, 12, 15],
       [1, 2, 5, 6, 9, 10, 13, 14]]
SLIST = [1, 1, 2, 2, 3, 3, 4, 4]

ENGS = ("tensor", "vector", "scalar", "sync", "gpsimd")
ESEM = {"tensor": "T", "vector": "V", "scalar": "S", "sync": "Y", "gpsimd": "G"}


class Buf:
    __slots__ = ("w", "readers")

    def __init__(self):
        self.w = None
        self.readers = {}


class Prog:
    def __init__(self, nc):
        self.nc = nc
        self.q = {e: [] for e in ENGS}
        self.sems = {}
        self.cnt = {}
        self.last_wait = {e: {} for e in ENGS}

    def sem(self, name):
        if name not in self.sems:
            self.sems[name] = self.nc.alloc_semaphore(name="sem_" + name)
            self.cnt[name] = 0
        return self.sems[name]

    def _wait(self, eng, sname, val):
        if val <= 0:
            return
        assert val <= self.cnt.get(sname, 0), (eng, sname, val, self.cnt.get(sname))
        lw = self.last_wait[eng]
        if lw.get(sname, 0) >= val:
            return
        lw[sname] = val
        h = self.sem(sname)
        self.q[eng].append(lambda e, h=h, v=val: e.wait_ge(h, v))

    def do(self, eng, build, reads=(), writes=(), dma=None, accum=False):
        my = ESEM[eng]
        for b in reads:
            if b.w:
                self._wait(eng, *b.w)
        for b in writes:
            if b.readers:
                for sn, v in b.readers.items():
                    self._wait(eng, sn, v)
            elif b.w and not accum:
                self._wait(eng, *b.w)
        sname = dma if dma else my
        inc = 16 if dma else 1
        h = self.sem(sname)
        self.cnt[sname] = self.cnt.get(sname, 0) + inc
        n = self.cnt[sname]
        self.q[eng].append(lambda e, b=build, h=h, i=inc: b(e).then_inc(h, i))
        for b in reads:
            if b.readers.get(sname, 0) < n:
                b.readers[sname] = n
        for b in writes:
            b.w = (sname, n)
            b.readers = {}
        return n

    def raw_wait(self, eng, sname, val):
        self._wait(eng, sname, val)

    def barrier(self, tag):
        vals = {}
        for e in ENGS:
            self._wait(e, ESEM[e], self.cnt.get(ESEM[e], 0))
            if e == "gpsimd":
                for sn in list(self.cnt):
                    if sn.startswith("ST") or sn == "GD" or sn.startswith("GL"):
                        self._wait(e, sn, self.cnt.get(sn, 0))
            if e == "sync":
                for sn in list(self.cnt):
                    if sn.startswith("L") or sn.startswith("SY"):
                        self._wait(e, sn, self.cnt.get(sn, 0))
            sn = "BAR" + ESEM[e]
            h = self.sem(sn)
            self.cnt[sn] = self.cnt.get(sn, 0) + 1
            vals[sn] = self.cnt[sn]
            self.q[e].append(lambda eng, h=h: eng.sem_inc(h, 1))
        for e in ENGS:
            for sn, v in vals.items():
                if sn != "BAR" + ESEM[e]:
                    self._wait(e, sn, v)

    def emit(self, block):
        for e in ENGS:
            ops = self.q[e]
            dec = getattr(block, e)

            def make(ops):
                def body(engine):
                    for f in ops:
                        f(engine)
                return body
            dec(make(ops))


def build_program(debug=(), stop_after=None):
    nc = bass.Bass("TRN2", target_bir_lowering=False)

    xq_d = nc.dram_tensor("xq", [8 * 128, D], F32, kind="ExternalInput")
    xn_d = nc.dram_tensor("xnat", [S, D], F32, kind="ExternalInput")
    lm_d = nc.dram_tensor("lastmask", [8, 128, 512], F32, kind="ExternalInput")
    cst_d = nc.dram_tensor("consts", [128, 576], F32, kind="ExternalInput")
    w_cq_d = nc.dram_tensor("wcqs", [128, 16 * 1536], F32, kind="ExternalInput")
    w_dqn_d = nc.dram_tensor("wdqns", [H, 128, 12 * 128], F32, kind="ExternalInput")
    w_dqr_d = nc.dram_tensor("wdqrs", [H, 128, 12 * 64], F32, kind="ExternalInput")
    w_ckv_d = nc.dram_tensor("wckvs", [128, 16 * 512], F32, kind="ExternalInput")
    w_dkn_d = nc.dram_tensor("wdkns", [H, 128, 4 * 128], F32, kind="ExternalInput")
    w_dv_d = nc.dram_tensor("wdvs4", [4, 128, 4 * 512], F32, kind="ExternalInput")
    w_kr_d = nc.dram_tensor("wkrs", [128, 16 * 64], F32, kind="ExternalInput")
    w_pr_d = nc.dram_tensor("wprs", [128, 16 * 2048], F32, kind="ExternalInput")
    w1t_d = nc.dram_tensor("w1s", [64, 128, 16 * 128], F32, kind="ExternalInput")
    w2t_d = nc.dram_tensor("w2t", [D_FF, D], F32, kind="ExternalInput")
    out_d = nc.dram_tensor("out", [8 * 128, D], F32, kind="ExternalOutput")
    dbg_d = {}
    for name, shape in debug:
        dbg_d[name] = nc.dram_tensor("dbg_" + name, shape, F32, kind="ExternalOutput")

    cqB = nc.dram_tensor("cqB", [2, 128, 16 * 1536], BF16)
    dqnB = nc.dram_tensor("dqnB", [2, H, 128, 12 * 128], BF16)
    dqrB = nc.dram_tensor("dqrB", [2, H, 128, 12 * 64], BF16)
    ckvB = nc.dram_tensor("ckvB", [2, 128, 16 * 512], BF16)
    dknB = nc.dram_tensor("dknB", [2, H, 128, 4 * 128], BF16)
    dvB = nc.dram_tensor("dvB", [2, 4, 128, 4 * 512], BF16)
    krB = nc.dram_tensor("krB", [2, 128, 16 * 64], BF16)
    prB = nc.dram_tensor("prB", [2, 128, 16 * 2048], BF16)
    w1B = nc.dram_tensor("w1B", [64, 128, 16 * 128], BF16)
    w2B = nc.dram_tensor("w2B", [D_FF, D], BF16)
    kkB = nc.dram_tensor("kkB", [2, H, 128, 2048], BF16)
    vvB = nc.dram_tensor("vvB", [2, 4, 128, 16 * 512], BF16)
    krotB = nc.dram_tensor("krotB", [2, H, 64, 2048], BF16)

    HCOLS = 22272
    BCOLS = 60672
    heap = nc.alloc_sbuf_tensor("heap", [128, HCOLS], F32)
    heapb = nc.alloc_sbuf_tensor("heapb", [128, BCOLS], BF16)
    int_t = nc.alloc_sbuf_tensor("int_t", [128, 512], I32)

    off = {}
    def region(name, cols):
        off[name] = region.p
        region.p += cols
    region.p = 0
    region("consts", 576)
    region("stats", 192)
    region("FP1", 4096)
    region("FPG", 8192)
    region("FPC", 6144)
    region("WS", 3072)
    assert region.p <= HCOLS, region.p

    offb = {}
    def regionb(name, cols):
        offb[name] = regionb.p
        regionb.p += cols
    regionb.p = 0
    regionb("XN", 16384)
    regionb("NQ", 12288)
    regionb("AB", 32000)
    assert regionb.p <= BCOLS, regionb.p

    def hs(name, r0, r1, c0, c1):
        return heap[r0:r1, off[name] + c0: off[name] + c1]

    def hv(name, c0, c1):
        return heap[:, off[name] + c0: off[name] + c1]

    def bs(name, r0, r1, c0, c1):
        return heapb[r0:r1, offb[name] + c0: offb[name] + c1]

    def bv(name, c0, c1):
        return heapb[:, offb[name] + c0: offb[name] + c1]

    C_ID, C_ONES, C_SWAP, C_GQ, C_GKV, C_COS, C_SIN, C_SGN, C_SPM, C_EPS, C_M30 = \
        0, 128, 256, 384, 396, 400, 416, 432, 433, 449, 450
    ident = hv("consts", C_ID, C_ID + 128)
    onesm = hv("consts", C_ONES, C_ONES + 128)
    swap64 = hs("consts", 0, 64, C_SWAP, C_SWAP + 64)

    ST_BN24, ST_BN2, ST_U, ST_MR, ST_MX, ST_DEN, ST_RC, ST_SCQ = 0, 24, 26, 27, 28, 36, 44, 48

    F_KROPE, F_KSWAP = 0, 2048
    F_Z, F_KF, F_GH, F_GL = 0, 512, 1024, 1536
    F_PROBS, F_QRF, F_QSF, F_QOF = 0, 2048, 2560, 3072
    F_SQ = 3584
    C_XT0, C_XT1, C_XNTM = 0, 2048, 4096
    C_KROTF = 0
    C_MASK = 0
    C_XTOK = 2048
    A_XTP = 0
    A_KRP = 16384
    A_DKN, A_DV4, A_KP, A_VP, A_KROTP = 6144, 8192, 12288, 16384, 17408
    A_AP, A_W1, A_W2 = 0, 8192, 14336
    A_CQ = 0
    A_K, A_V, A_KROT, A_QN, A_QROT, A_PT, A_DQN, A_DQR = \
        0, 8192, 16384, 24576, 25600, 26624, 27392, 30464
    A_PR = 0

    banks = [nc.alloc_psum_tensor("pb%d" % i, [128, 512], F32) for i in range(8)]

    p = Prog(nc)
    bufs = {}

    def BF(name):
        if name not in bufs:
            bufs[name] = Buf()
        return bufs[name]
    pbank = [BF("psum%d" % i) for i in range(8)]

    LOAD_SLOTS = ["L%d" % i for i in range(7)]
    slot_i = [0]

    def next_slot():
        sl = LOAD_SLOTS[slot_i[0] % len(LOAD_SLOTS)]
        slot_i[0] += 1
        return sl

    def load(dst_ap, src_ap, dstbuf, srcextra=(), slot=None, extra_writes=()):
        sl = slot if slot else next_slot()
        p.raw_wait("sync", sl, p.cnt.get(sl, 0))
        return p.do("sync", lambda e, o=dst_ap, i=src_ap: e.dma_start(out=o, in_=i),
                    reads=list(srcextra), writes=[dstbuf] + list(extra_writes), dma=sl)

    gl_i = [0]

    def gload(dst_ap, src_ap, dstbuf, srcextra=()):
        sem = "GL%d" % (gl_i[0] % 3)
        gl_i[0] += 1
        p.raw_wait("gpsimd", sem, p.cnt.get(sem, 0))
        return p.do("gpsimd", lambda e, o=dst_ap, i=src_ap: e.dma_start(out=o, in_=i),
                    reads=list(srcextra), writes=[dstbuf], dma=sem)

    st_i = [0]

    def store(dst_ap, src_ap, srcbuf, q="gpsimd", sem=None):
        if sem is None:
            sem = ("ST%d" % (st_i[0] % 4)) if q == "gpsimd" else ("SY%d" % (st_i[0] % 4))
            st_i[0] += 1
        p.raw_wait(q, sem, p.cnt.get(sem, 0))
        return p.do(q, lambda e, o=dst_ap, i=src_ap: e.dma_start(out=o, in_=i),
                    reads=[srcbuf], writes=[], dma=sem)

    def tap(name, src_ap, srcbuf):
        if name in dbg_d:
            store(dbg_d[name][:, :], src_ap, srcbuf)
            for sn in list(p.cnt):
                if sn.startswith("ST"):
                    p.raw_wait("gpsimd", sn, p.cnt.get(sn, 0))

    def stop_here(tag):
        if stop_after == tag:
            for sn in list(p.cnt):
                if sn.startswith("ST"):
                    p.raw_wait("gpsimd", sn, p.cnt.get(sn, 0))
                if sn.startswith("SY"):
                    p.raw_wait("sync", sn, p.cnt.get(sn, 0))
            with nc.Block() as block:
                p.emit(block)
            return True
        return False

    cstb = BF("consts")
    load(hv("consts", 0, 576), cst_d[:, :], cstb)
    sgn_ap = hs("consts", 0, 64, C_SGN, C_SGN + 1)
    p.do("vector", lambda e: e.tensor_scalar_mul(
        hs("consts", 0, 64, C_SPM, C_SPM + 16), hs("consts", 0, 64, C_SIN, C_SIN + 16), sgn_ap),
        reads=[cstb], writes=[cstb])

    wsplit_f = BF("wsplit_f")
    wsplit_h = BF("wsplit_h")
    wsplit_l = BF("wsplit_l")

    wsplit_f2 = BF("wsplit_f2"); wsplit_h2 = BF("wsplit_h2"); wsplit_l2 = BF("wsplit_l2")
    split_par = [0]

    def split_weight(src_fp32, dst_hi, dst_lo, ncols, late=False):
        par = split_par[0] % 2
        split_par[0] += 1
        assert ncols <= 1536
        fb = wsplit_f if par == 0 else wsplit_f2
        hb = wsplit_h if par == 0 else wsplit_h2
        lb = wsplit_l if par == 0 else wsplit_l2
        ao = 27904
        fa = hv("WS", 1536 * par, 1536 * par + ncols)
        hi = bv("AB", ao + 2048 * par, ao + 2048 * par + ncols)
        lo = bv("AB", ao - 4096 + 2048 * par, ao - 4096 + 2048 * par + ncols)
        load(fa, src_fp32, fb)
        gload(hi, src_fp32, hb)
        eng = "gpsimd" if par == 0 else "vector"
        p.do(eng, lambda e, fa=fa, hi=hi, lo=lo: e.tensor_tensor(lo, fa, hi, ALU.subtract),
             reads=[fb, hb], writes=[lb])
        store(dst_lo, lo, lb)

    def cast_weight(src_fp32, dst_bf16):
        p.do("gpsimd", lambda e, o=dst_bf16, i=src_fp32: e.dma_start(out=o, in_=i),
             reads=[], writes=[], dma="GD")

    deferred_splits = []
    p1_splits_pending = []
    hi_casts = []
    cast_weight(w_ckv_d[:, :], ckvB[0])
    cast_weight(w_kr_d[:, :], krB[0])
    for kc in range(16):
        deferred_splits.append((w_cq_d[:, 1536 * kc:1536 * (kc + 1)],
                                None, cqB[1, :, 1536 * kc:1536 * (kc + 1)], 1536))
    for h in range(H):
        deferred_splits.append((w_dqn_d[h], None, dqnB[1, h], 1536))
        deferred_splits.append((w_dqr_d[h], None, dqrB[1, h], 768))
        p1_splits_pending.append((w_dkn_d[h], None, dknB[1, h], 512))
    for hg in range(4):
        for j2 in range(2):
            p1_splits_pending.append((w_dv_d[hg][:, 1024 * j2:1024 * (j2 + 1)],
                                      None, dvB[1, hg][:, 1024 * j2:1024 * (j2 + 1)], 1024))
    for kc in range(16):
        for j2 in range(2):
            deferred_splits.append((w_pr_d[:, 2048 * kc + 1024 * j2:2048 * kc + 1024 * (j2 + 1)],
                                    None, prB[1, :, 2048 * kc + 1024 * j2:2048 * kc + 1024 * (j2 + 1)], 1024))
    for c2 in range(8):
        split_weight(w_ckv_d[:, 1024 * c2:1024 * (c2 + 1)],
                     None, ckvB[1, :, 1024 * c2:1024 * (c2 + 1)], 1024)
    split_weight(w_kr_d[:, :], None, krB[1], 1024)
    p.barrier("Pm1")
    cast_weight(w_dkn_d[:, :, :], dknB[0])
    cast_weight(w_dv_d[:, :, :], dvB[0])

    def cast_gx(src_fp32, dst_bf16, sem):
        p.do("gpsimd", lambda e, o=dst_bf16, i=src_fp32: e.dma_start(out=o, in_=i),
             reads=[], writes=[], dma=sem)

    gx_p1 = []
    for fg in range(0, 64, 4):
        gx_p1.append((w1t_d[fg:fg + 4], w1B[fg:fg + 4], "GXA" if fg < 32 else "GXB"))
    gx_p1end = []
    for i2 in range(16):
        r2 = 512 * i2
        gx_p1end.append((w2t_d[r2:r2 + 512, :], w2B[r2:r2 + 512, :], "GXC" if i2 < 8 else "GXD"))
    gx_p2a = []
    for i2 in range(4):
        gx_p2a.append((w_cq_d[32 * i2:32 * (i2 + 1), :], cqB[0, 32 * i2:32 * (i2 + 1), :], "GXE"))
    for h2 in range(0, 16, 4):
        gx_p2a.append((w_dqn_d[h2:h2 + 4], dqnB[0, h2:h2 + 4], "GXE"))
    gx_p2a.append((w_dqr_d[0:8], dqrB[0, 0:8], "GXE"))
    gx_p2a.append((w_dqr_d[8:16], dqrB[0, 8:16], "GXE"))
    gx_p2b = []
    for i2 in range(8):
        gx_p2b.append((w_pr_d[16 * i2:16 * (i2 + 1), :], prB[0, 16 * i2:16 * (i2 + 1), :], "GXF"))
    if stop_here("PW"):
        return nc

    def mm3w(bank_ap, bankbuf, wh_ap, wl_ap, xh_ap, xl_ap, wbufs, xbufs, first, last, n=None):
        p.do("tensor", lambda e: e.matmul(bank_ap, wh_ap, xh_ap, start=first, stop=False),
             reads=list(wbufs) + list(xbufs), writes=[pbank[n]], accum=not first)
        p.do("tensor", lambda e: e.matmul(bank_ap, wh_ap, xl_ap, start=False, stop=False),
             reads=list(wbufs) + list(xbufs), writes=[pbank[n]], accum=True)
        p.do("tensor", lambda e: e.matmul(bank_ap, wl_ap, xh_ap, start=False, stop=last),
             reads=list(wbufs) + list(xbufs), writes=[pbank[n]], accum=True)

    def evac_pair(bank_ap, nb, hi_ap, lo_ap, hibuf, lobuf, scale=None):
        if scale is None:
            p.do("scalar", lambda e: e.activation(hi_ap, bank_ap, AF.Copy),
                 reads=[pbank[nb]], writes=[hibuf])
            p.do("vector", lambda e: e.tensor_tensor(lo_ap, bank_ap, hi_ap, ALU.subtract),
                 reads=[pbank[nb], hibuf], writes=[lobuf])
        else:
            p.do("scalar", lambda e: e.activation(hi_ap, bank_ap, AF.Copy, scale=scale),
                 reads=[pbank[nb]], writes=[hibuf])
            p.do("vector", lambda e: e.scalar_tensor_tensor(
                lo_ap, bank_ap, scale, hi_ap, ALU.mult, ALU.subtract),
                reads=[pbank[nb], hibuf], writes=[lobuf])

    xt = [[BF("xt%d_%d" % (par, c)) for c in range(4)] for par in range(2)]
    xntmp = BF("xntmp")
    xtpb = BF("xtp")
    nkvf = BF("nkvf")
    nkvb = BF("nkvb")
    krop = BF("krope"); kswb = BF("kswap"); statb = BF("stats")
    ckvb = BF("ckvb"); krb = BF("krb")
    sskvb = BF("sskv")

    statp = [BF("stats0"), BF("stats1")]
    xntmp2 = BF("xntmp2")
    xnt_q = [BF("xntq%d" % c) for c in range(4)]
    tb_rot = [0]
    TBANKS = [0, 1, 2, 5, 6, 7]

    def layernorm_tile(xcols, xbuf, hi_fn, lo_fn, dsthi, dstlo, par=0):
        xa = hv("FPC", xcols, xcols + 2048)
        SO = 96 * par
        sb = statp[par]
        xnt = hv("FPC", C_XNTM, C_XNTM + 2048)
        xntb = xntmp
        for c4 in range(4):
            p.do("vector", lambda e, c4=c4, xa=xa, SO=SO: e.bn_stats(
                hv("stats", SO + ST_BN24 + 6 * c4, SO + ST_BN24 + 6 * c4 + 6), xa[:, 512 * c4:512 * (c4 + 1)]),
                reads=[xbuf[c4]], writes=[sb])
        p.do("vector", lambda e, SO=SO: e.bn_aggr(hv("stats", SO + ST_BN2, SO + ST_BN2 + 2),
                                           hv("stats", SO + ST_BN24, SO + ST_BN24 + 24)),
             reads=[sb], writes=[sb])
        p.do("scalar", lambda e, SO=SO: e.activation(hv("stats", SO + ST_U, SO + ST_U + 1),
                                              hv("stats", SO + ST_BN2 + 1, SO + ST_BN2 + 2),
                                              AF.Sqrt, bias=hv("consts", C_EPS, C_EPS + 1),
                                              scale=1.0),
             reads=[sb, cstb], writes=[sb])
        p.do("vector", lambda e, SO=SO: e.reciprocal(hv("stats", SO + ST_U, SO + ST_U + 1),
                                              hv("stats", SO + ST_U, SO + ST_U + 1)),
             reads=[sb], writes=[sb])
        p.do("vector", lambda e, SO=SO: e.tensor_scalar(
            hv("stats", SO + ST_MR, SO + ST_MR + 1), hv("stats", SO + ST_BN2, SO + ST_BN2 + 1),
            hv("stats", SO + ST_U, SO + ST_U + 1), -1.0, ALU.mult, ALU.mult),
            reads=[sb], writes=[sb])
        for c in range(4):
            p.do("scalar", lambda e, xa=xa, SO=SO, xnt=xnt, c=c: e.activation(
                xnt[:, 512 * c:512 * (c + 1)], xa[:, 512 * c:512 * (c + 1)], AF.Identity,
                bias=hv("stats", SO + ST_MR, SO + ST_MR + 1), scale=hv("stats", SO + ST_U, SO + ST_U + 1)),
                reads=[xbuf[c], sb], writes=[xnt_q[c]])
        for c in range(4):
            bk = TBANKS[tb_rot[0] % 6]
            tb_rot[0] += 1
            for q in range(4):
                p.do("tensor", lambda e, c=c, q=q, bk=bk, xnt=xnt: e.transpose(
                    banks[bk][:, 128 * q:128 * (q + 1)],
                    xnt[:, 512 * c + 128 * q:512 * c + 128 * (q + 1)], ident),
                    reads=[xnt_q[c], cstb], writes=[pbank[bk]], accum=(q > 0))
            evac_pair(banks[bk][:, 0:512], bk, hi_fn(c), lo_fn(c), dsthi, dstlo)

    _s = next_slot()
    load(bv("XN", 0, 8192), ckvB[0], ckvb, slot=_s)
    load(bv("XN", 8192, 16384), ckvB[1], ckvb, slot=_s)
    _s = next_slot()
    load(bv("AB", A_KRP, A_KRP + 1024), krB[0], krb, slot=_s)
    load(bv("AB", A_KRP + 1024, A_KRP + 2048), krB[1], krb, slot=_s)

    p1_splits = list(p1_splits_pending)
    for grp in range(4):
        for j in range(4):
            t = grp * 4 + j
            xb = xt[t % 2]
            for c in range(4):
                load(hv("FPC", 2048 * (t % 2) + 512 * c, 2048 * (t % 2) + 512 * (c + 1)),
                     xn_d[128 * t:128 * (t + 1), 512 * c:512 * (c + 1)], xb[c])
            layernorm_tile(2048 * (t % 2), xb,
                           lambda c, j=j: bv("AB", A_XTP + 2048 * c, A_XTP + 2048 * (c + 1))
                           .rearrange("p (k m) -> p k m", m=512)[:, :, 128 * j:128 * (j + 1)],
                           lambda c, j=j: bv("AB", A_XTP + 8192 + 2048 * c, A_XTP + 8192 + 2048 * (c + 1))
                           .rearrange("p (k m) -> p k m", m=512)[:, :, 128 * j:128 * (j + 1)],
                           xtpb, xtpb, par=t % 2)
            if t >= 4 and gx_p1:
                sx, dx, semx = gx_p1.pop(0)
                cast_gx(sx, dx, semx)
        for mc in range(4):
            bk = mc
            for kc in range(16):
                mm3w(banks[bk][:, 0:512], None,
                     bv("XN", 512 * kc + 128 * mc, 512 * kc + 128 * (mc + 1)),
                     bv("XN", 8192 + 512 * kc + 128 * mc, 8192 + 512 * kc + 128 * (mc + 1)),
                     bv("AB", A_XTP + 512 * kc, A_XTP + 512 * (kc + 1)),
                     bv("AB", A_XTP + 8192 + 512 * kc, A_XTP + 8192 + 512 * (kc + 1)),
                     [ckvb], [xtpb], kc == 0, kc == 15, n=bk)
            p.do("vector", lambda e, mc=mc, grp=grp, bk=bk: e.tensor_copy(
                hv("FPG", 2048 * mc + 512 * grp, 2048 * mc + 512 * (grp + 1)), banks[bk][:, 0:512]),
                reads=[pbank[bk]], writes=[nkvf])
            p.do("scalar", lambda e, mc=mc, grp=grp: e.activation(
                hv("FPC", C_XNTM, C_XNTM + 512),
                hv("FPG", 2048 * mc + 512 * grp, 2048 * mc + 512 * (grp + 1)), AF.Square),
                reads=[nkvf], writes=[xntmp])
            p.do("tensor", lambda e, mc=mc: e.matmul(
                banks[6][0:1, 0:512], hv("consts", C_ONES, C_ONES + 1),
                hv("FPC", C_XNTM, C_XNTM + 512),
                start=(mc == 0), stop=(mc == 3)),
                reads=[xntmp, cstb], writes=[pbank[6]], accum=(mc != 0))
        p.do("vector", lambda e, grp=grp: e.tensor_copy(
            hs("FP1", 64, 65, F_KROPE + 512 * grp, F_KROPE + 512 * (grp + 1)), banks[6][0:1, 0:512]),
            reads=[pbank[6]], writes=[sskvb])
        for kc in range(16):
            p.do("tensor", lambda e, kc=kc: e.matmul(
                banks[4][0:64, 0:512], bv("AB", A_KRP + 64 * kc, A_KRP + 64 * (kc + 1)),
                bv("AB", A_XTP + 512 * kc, A_XTP + 512 * (kc + 1)),
                start=(kc == 0), stop=False),
                reads=[krb, xtpb], writes=[pbank[4]], accum=(kc != 0))
            p.do("tensor", lambda e, kc=kc: e.matmul(
                banks[4][0:64, 0:512], bv("AB", A_KRP + 64 * kc, A_KRP + 64 * (kc + 1)),
                bv("AB", A_XTP + 8192 + 512 * kc, A_XTP + 8192 + 512 * (kc + 1)),
                start=False, stop=False),
                reads=[krb, xtpb], writes=[pbank[4]], accum=True)
            p.do("tensor", lambda e, kc=kc: e.matmul(
                banks[4][0:64, 0:512], bv("AB", A_KRP + 1024 + 64 * kc, A_KRP + 1024 + 64 * (kc + 1)),
                bv("AB", A_XTP + 512 * kc, A_XTP + 512 * (kc + 1)),
                start=False, stop=(kc == 15)),
                reads=[krb, xtpb], writes=[pbank[4]], accum=True)
        p.do("vector", lambda e, grp=grp: e.tensor_copy(
            hs("FP1", 0, 64, F_KROPE + 512 * grp, F_KROPE + 512 * (grp + 1)), banks[4][0:64, 0:512]),
            reads=[pbank[4]], writes=[krop])
        for _ in range(6):
            if p1_splits:
                sw = p1_splits.pop(0)
                split_weight(sw[0], sw[1], sw[2], sw[3], late=True)

    p.do("scalar", lambda e: e.activation(hs("FP1", 64, 65, 0, 2048), hs("FP1", 64, 65, 0, 2048),
                                          AF.Sqrt, bias=hs("consts", 64, 65, C_EPS, C_EPS + 1),
                                          scale=1.0 / R_KV),
         reads=[sskvb, cstb], writes=[sskvb])
    p.do("vector", lambda e: e.reciprocal(hs("FP1", 64, 65, 0, 2048), hs("FP1", 64, 65, 0, 2048)),
         reads=[sskvb], writes=[sskvb])
    for c4 in range(4):
        p.do("tensor", lambda e, c4=c4: e.matmul(
            banks[7][:, 0:512], hs("consts", 64, 65, C_ONES, C_ONES + 128), hs("FP1", 64, 65, 512 * c4, 512 * (c4 + 1)),
            start=True, stop=True),
            reads=[sskvb, cstb], writes=[pbank[7]])
        for mc in range(4):
            p.do("vector", lambda e, c4=c4, mc=mc: e.tensor_tensor(
                hv("FPG", 2048 * mc + 512 * c4, 2048 * mc + 512 * (c4 + 1)),
                hv("FPG", 2048 * mc + 512 * c4, 2048 * mc + 512 * (c4 + 1)),
                banks[7][:, 0:512], ALU.mult),
                reads=[pbank[7], nkvf], writes=[nkvf])
    for mc in range(4):
        p.do("scalar", lambda e, mc=mc: e.activation(
            bv("XN", 2048 * mc, 2048 * (mc + 1)), hv("FPG", 2048 * mc, 2048 * (mc + 1)),
            AF.Copy, scale=hv("consts", C_GKV + mc, C_GKV + mc + 1)),
            reads=[nkvf, cstb], writes=[nkvb])
        p.do("vector", lambda e, mc=mc: e.scalar_tensor_tensor(
            bv("XN", 8192 + 2048 * mc, 8192 + 2048 * (mc + 1)),
            hv("FPG", 2048 * mc, 2048 * (mc + 1)),
            hv("consts", C_GKV + mc, C_GKV + mc + 1),
            bv("XN", 2048 * mc, 2048 * (mc + 1)),
            ALU.mult, ALU.subtract),
            reads=[nkvf, nkvb, cstb], writes=[nkvb])
    for c4 in range(4):
        p.do("tensor", lambda e, c4=c4: e.matmul(
            banks[5][0:64, 0:512], swap64, hs("FP1", 0, 64, F_KROPE + 512 * c4, F_KROPE + 512 * (c4 + 1)),
            start=True, stop=True),
            reads=[krop, cstb], writes=[pbank[5]])
        p.do("vector", lambda e, c4=c4: e.tensor_copy(
            hs("FP1", 0, 64, F_KSWAP + 512 * c4, F_KSWAP + 512 * (c4 + 1)), banks[5][0:64, 0:512]),
            reads=[pbank[5]], writes=[kswb])
    p.barrier("P1")
    if stop_here("P1"):
        return nc

    dknp = [BF("dknp0"), BF("dknp1")]; dv4p = [BF("dv4p0"), BF("dv4p1")]
    kpb = BF("kpb"); vpb = BF("vpb"); krotf = BF("krotf"); krotp = BF("krotp")

    def load_dkn(h):
        pr2 = h % 2
        _s = next_slot()
        load(bv("AB", A_DKN + 1024 * pr2, A_DKN + 1024 * pr2 + 512), dknB[0, h], dknp[pr2], slot=_s)
        load(bv("AB", A_DKN + 1024 * pr2 + 512, A_DKN + 1024 * pr2 + 1024), dknB[1, h], dknp[pr2], slot=_s)

    def dv_base(hg):
        return A_DV4 if hg % 2 == 0 else 0

    def load_dv(hg):
        b0 = dv_base(hg)
        _s = next_slot()
        load(bv("AB", b0, b0 + 2048), dvB[0, hg], dv4p[hg % 2], slot=_s)
        load(bv("AB", b0 + 2048, b0 + 4096), dvB[1, hg], dv4p[hg % 2], slot=_s)

    load_dkn(0)
    load_dv(0)
    for h in range(H):
        pr2 = h % 2
        if h + 1 < H:
            load_dkn(h + 1)
        if h % 4 == 0 and h + 4 < H:
            load_dv(h // 4 + 1)
        for ng in range(4):
            bk = ng % 2
            for kc in range(4):
                mm3w(banks[bk][:, 0:512], None,
                     bv("AB", A_DKN + 1024 * pr2 + 128 * kc, A_DKN + 1024 * pr2 + 128 * (kc + 1)),
                     bv("AB", A_DKN + 1024 * pr2 + 512 + 128 * kc, A_DKN + 1024 * pr2 + 512 + 128 * (kc + 1)),
                     bv("XN", 2048 * kc + 512 * ng, 2048 * kc + 512 * (ng + 1)),
                     bv("XN", 8192 + 2048 * kc + 512 * ng, 8192 + 2048 * kc + 512 * (ng + 1)),
                     [dknp[pr2]], [nkvb], kc == 0, kc == 3, n=bk)
            evac_pair(banks[bk][:, 0:512], bk,
                      bv("AB", A_KP + 512 * ng, A_KP + 512 * (ng + 1)),
                      bv("AB", A_KP + 2048 + 512 * ng, A_KP + 2048 + 512 * (ng + 1)),
                      kpb, kpb)
        store(kkB[0, h], bv("AB", A_KP, A_KP + 2048), kpb, q="sync")
        store(kkB[1, h], bv("AB", A_KP + 2048, A_KP + 4096), kpb)
        if h % 4 == 0:
            hg = h // 4
            for kt in range(16):
                bk = 2 + kt % 2
                for kc in range(4):
                    b0 = dv_base(hg)
                    mm3w(banks[bk][:, 0:512], None,
                         bv("XN", 2048 * kc + 128 * kt, 2048 * kc + 128 * (kt + 1)),
                         bv("XN", 8192 + 2048 * kc + 128 * kt, 8192 + 2048 * kc + 128 * (kt + 1)),
                         bv("AB", b0 + 512 * kc, b0 + 512 * (kc + 1)),
                         bv("AB", b0 + 2048 + 512 * kc, b0 + 2048 + 512 * (kc + 1)),
                         [nkvb], [dv4p[hg % 2]], kc == 0, kc == 3, n=bk)
                evac_pair(banks[bk][:, 0:512], bk,
                          bv("AB", A_VP, A_VP + 512),
                          bv("AB", A_VP + 512, A_VP + 1024),
                          vpb, vpb, scale=V_SCALE)
                store(vvB[0, hg][:, 512 * kt:512 * (kt + 1)], bv("AB", A_VP, A_VP + 512),
                      vpb, q="sync")
                store(vvB[1, hg][:, 512 * kt:512 * (kt + 1)], bv("AB", A_VP + 512, A_VP + 1024), vpb)
        cos_h = hs("consts", 0, 64, C_COS + h, C_COS + h + 1)
        spm_h = hs("consts", 0, 64, C_SPM + h, C_SPM + h + 1)
        p.do("vector", lambda e, cos_h=cos_h: e.tensor_scalar_mul(
            hs("FPC", 0, 64, C_KROTF, C_KROTF + 2048), hs("FP1", 0, 64, F_KROPE, F_KROPE + 2048), cos_h),
            reads=[krop, cstb], writes=[krotf])
        p.do("vector", lambda e, spm_h=spm_h: e.scalar_tensor_tensor(
            hs("FPC", 0, 64, C_KROTF, C_KROTF + 2048), hs("FP1", 0, 64, F_KSWAP, F_KSWAP + 2048), spm_h,
            hs("FPC", 0, 64, C_KROTF, C_KROTF + 2048), ALU.mult, ALU.add),
            reads=[kswb, cstb, krotf], writes=[krotf])
        p.do("scalar", lambda e: e.activation(
            bs("AB", 0, 64, A_KROTP, A_KROTP + 2048), hs("FPC", 0, 64, C_KROTF, C_KROTF + 2048),
            AF.Copy),
            reads=[krotf], writes=[krotp])
        p.do("vector", lambda e: e.tensor_tensor(
            bs("AB", 0, 64, A_KROTP + 2048, A_KROTP + 4096),
            hs("FPC", 0, 64, C_KROTF, C_KROTF + 2048),
            bs("AB", 0, 64, A_KROTP, A_KROTP + 2048), ALU.subtract),
            reads=[krotf, krotp], writes=[krotp])
        store(krotB[0, h], bs("AB", 0, 64, A_KROTP, A_KROTP + 2048), krotp, q="sync")
        store(krotB[1, h], bs("AB", 0, 64, A_KROTP + 2048, A_KROTP + 4096), krotp)
        if gx_p1:
            sx, dx, semx = gx_p1.pop(0)
            cast_gx(sx, dx, semx)
        elif gx_p1end and h < 12:
            sx, dx, semx = gx_p1end.pop(0)
            cast_gx(sx, dx, semx)
    p.barrier("P1end")
    if stop_here("P1end"):
        return nc

    xnown = BF("xn_own")
    outfm = BF("out_fm")
    nqb = BF("nq")
    gb = BF("g")
    w1buf = [BF("w1b%d" % i) for i in range(3)]
    w2buf = [BF("w2b%d" % i) for i in range(3)]
    cqbuf = [BF("cqb%d" % i) for i in range(3)]
    prbuf = [BF("prb%d" % i) for i in range(3)]
    rndb = BF("rnd")
    aslab_h = [BF("a_h%d" % i) for i in range(8)]
    aslab_l = [BF("a_l%d" % i) for i in range(8)]
    maskb = BF("maskb")
    kvb = [BF("kv0"), BF("kv1")]
    dqp = BF("dqp")
    qnb = BF("qnb"); qrf = BF("qrf"); qrotb = BF("qrotb")
    probs = BF("probs"); ptb = [BF("pt%d" % i) for i in range(3)]
    sgb = BF("sg"); obuf = BF("obuf"); xtok = BF("xtok")
    rqrow = BF("rqrow"); scqb = BF("scq")

    for g in range(2):
        slots = [g * 4 + i for i in range(4)]
        svals = [SLIST[sl] for sl in slots]

        for i in range(4):
            sl = g * 4 + i
            xb = xt[i % 2]
            for c in range(4):
                load(hv("FPC", 2048 * (i % 2) + 512 * c, 2048 * (i % 2) + 512 * (c + 1)),
                     xq_d[128 * sl:128 * (sl + 1), 512 * c:512 * (c + 1)], xb[c])
            layernorm_tile(2048 * (i % 2), xb,
                           lambda c, i=i: bv("XN", 2048 * c, 2048 * (c + 1))
                           .rearrange("p (k m) -> p k m", m=512)[:, :, 128 * i:128 * (i + 1)],
                           lambda c, i=i: bv("XN", 8192 + 2048 * c, 8192 + 2048 * (c + 1))
                           .rearrange("p (k m) -> p k m", m=512)[:, :, 128 * i:128 * (i + 1)],
                           xnown, xnown)
        p.barrier("P1b_%d" % g)
        if stop_here("P1b_%d" % g):
            return nc

        for slab in range(8):
            F1BANKS = [0, 1, 6, 7]

            def w1_load(fg):
                load(bv("AB", A_W1 + 2048 * (fg % 3), A_W1 + 2048 * (fg % 3) + 2048),
                     w1B[fg, :, :], w1buf[fg % 3])
            if slab == 0:
                if g == 0:
                    p.raw_wait("sync", "GXA", p.cnt.get("GXA", 0))
                w1_load(0)
                w1_load(1)
            for fc in range(8):
                fg = slab * 8 + fc
                wb = w1buf[fg % 3]
                if fg + 2 < 64:
                    if g == 0 and fg + 2 == 32:
                        p.raw_wait("sync", "GXB", p.cnt.get("GXB", 0))
                    w1_load(fg + 2)
                bk = F1BANKS[fc % 4]
                rb = BF("rnd%d" % (fc % 4))
                RQH = ("FP1", 0) if fc % 4 == 0 else ("FP1", 2048) if fc % 4 == 1 else \
                      ("FPC", 0) if fc % 4 == 2 else ("FPC", 2048)
                for kc in range(16):
                    p.do("tensor", lambda e, kc=kc, fg=fg, bk=bk: e.matmul(
                        banks[bk][:, 0:512],
                        bv("AB", A_W1 + 2048 * (fg % 3) + 128 * kc, A_W1 + 2048 * (fg % 3) + 128 * (kc + 1)),
                        bv("XN", 512 * kc, 512 * (kc + 1)),
                        start=(kc == 0), stop=False),
                        reads=[wb, xnown], writes=[pbank[bk]], accum=(kc != 0))
                    p.do("tensor", lambda e, kc=kc, fg=fg, bk=bk: e.matmul(
                        banks[bk][:, 0:512],
                        bv("AB", A_W1 + 2048 * (fg % 3) + 128 * kc, A_W1 + 2048 * (fg % 3) + 128 * (kc + 1)),
                        bv("XN", 8192 + 512 * kc, 8192 + 512 * (kc + 1)),
                        start=False, stop=(kc == 15)),
                        reads=[wb, xnown], writes=[pbank[bk]], accum=True)
                p.do("scalar", lambda e, bk=bk, RQH=RQH: e.activation(
                    hv(RQH[0], RQH[1] + F_Z, RQH[1] + F_Z + 512), banks[bk][:, 0:512], AF.Copy, scale=10.0),
                    reads=[pbank[bk]], writes=[rb])
                it = int_t
                itb = BF("int_t0")
                p.do("vector", lambda e, RQH=RQH, it=it: e.tensor_copy(
                    it[:, :], hv(RQH[0], RQH[1] + F_Z, RQH[1] + F_Z + 512)),
                    reads=[rb, itb], writes=[itb])
                p.do("vector", lambda e, RQH=RQH, it=it: e.tensor_copy(
                    hv(RQH[0], RQH[1] + F_KF, RQH[1] + F_KF + 512), it[:, :]),
                    reads=[itb], writes=[rb])
                p.do("vector", lambda e, RQH=RQH: e.scalar_tensor_tensor(
                    hv(RQH[0], RQH[1] + F_GH, RQH[1] + F_GH + 512), hv(RQH[0], RQH[1] + F_Z, RQH[1] + F_Z + 512), 0.5,
                    hv(RQH[0], RQH[1] + F_KF, RQH[1] + F_KF + 512), ALU.subtract, ALU.is_gt),
                    reads=[rb], writes=[rb])
                p.do("vector", lambda e, RQH=RQH: e.scalar_tensor_tensor(
                    hv(RQH[0], RQH[1] + F_GL, RQH[1] + F_GL + 512), hv(RQH[0], RQH[1] + F_Z, RQH[1] + F_Z + 512), 0.5,
                    hv(RQH[0], RQH[1] + F_KF, RQH[1] + F_KF + 512), ALU.add, ALU.is_lt),
                    reads=[rb], writes=[rb])
                p.do("vector", lambda e, RQH=RQH: e.tensor_sub(
                    hv(RQH[0], RQH[1] + F_GH, RQH[1] + F_GH + 512), hv(RQH[0], RQH[1] + F_GH, RQH[1] + F_GH + 512),
                    hv(RQH[0], RQH[1] + F_GL, RQH[1] + F_GL + 512)),
                    reads=[rb], writes=[rb])
                p.do("vector", lambda e, RQH=RQH: e.tensor_add(
                    hv(RQH[0], RQH[1] + F_KF, RQH[1] + F_KF + 512), hv(RQH[0], RQH[1] + F_KF, RQH[1] + F_KF + 512),
                    hv(RQH[0], RQH[1] + F_GH, RQH[1] + F_GH + 512)),
                    reads=[rb], writes=[rb])
                p.do("vector", lambda e, RQH=RQH: e.tensor_scalar(
                    hv(RQH[0], RQH[1] + F_KF, RQH[1] + F_KF + 512), hv(RQH[0], RQH[1] + F_KF, RQH[1] + F_KF + 512),
                    QMAX, QMIN, ALU.min, ALU.max),
                    reads=[rb], writes=[rb])
                p.do("scalar", lambda e, RQH=RQH: e.activation(
                    hv(RQH[0], RQH[1] + F_Z, RQH[1] + F_Z + 512), hv(RQH[0], RQH[1] + F_KF, RQH[1] + F_KF + 512),
                    AF.Gelu_apprx_tanh, scale=0.1),
                    reads=[rb], writes=[rb])
                p.do("scalar", lambda e, fc=fc, RQH=RQH: e.activation(
                    bv("AB", A_AP + 512 * fc, A_AP + 512 * (fc + 1)),
                    hv(RQH[0], RQH[1] + F_Z, RQH[1] + F_Z + 512), AF.Copy),
                    reads=[rb], writes=[aslab_h[fc]])
                p.do("gpsimd", lambda e, fc=fc, RQH=RQH: e.tensor_tensor(
                    bv("AB", A_AP + 4096 + 512 * fc, A_AP + 4096 + 512 * (fc + 1)),
                    hv(RQH[0], RQH[1] + F_Z, RQH[1] + F_Z + 512),
                    bv("AB", A_AP + 512 * fc, A_AP + 512 * (fc + 1)), ALU.subtract),
                    reads=[rb, aslab_h[fc]], writes=[aslab_l[fc]])
                if g == 0 and deferred_splits:
                    sw = deferred_splits.pop(0)
                    split_weight(sw[0], sw[1], sw[2], sw[3], late=True)
            for nt in range(4):
                def w2_load(kc, nt=nt):
                    fg2 = slab * 8 + kc
                    load(bv("AB", A_W2 + 512 * (kc % 3), A_W2 + 512 * (kc % 3) + 512),
                         w2B[128 * fg2:128 * (fg2 + 1), 512 * nt:512 * (nt + 1)], w2buf[kc % 3])
                if g == 0 and slab == 0 and nt == 0:
                    p.raw_wait("sync", "GXC", p.cnt.get("GXC", 0))
                if g == 0 and slab == 4 and nt == 0:
                    p.raw_wait("sync", "GXD", p.cnt.get("GXD", 0))
                w2_load(0)
                w2_load(1)
                for kc in range(8):
                    wb = w2buf[kc % 3]
                    if kc + 2 < 8:
                        w2_load(kc + 2)
                    for tt in range(4):
                        p.do("tensor", lambda e, kc=kc, tt=tt: e.matmul(
                            banks[2 + tt][:, 0:512],
                            bv("AB", A_AP + 512 * kc + 128 * tt, A_AP + 512 * kc + 128 * (tt + 1)),
                            bv("AB", A_W2 + 512 * (kc % 3), A_W2 + 512 * (kc % 3) + 512),
                            start=(kc == 0), stop=False),
                            reads=[aslab_h[kc], wb], writes=[pbank[2 + tt]], accum=(kc != 0))
                        p.do("tensor", lambda e, kc=kc, tt=tt: e.matmul(
                            banks[2 + tt][:, 0:512],
                            bv("AB", A_AP + 4096 + 512 * kc + 128 * tt, A_AP + 4096 + 512 * kc + 128 * (tt + 1)),
                            bv("AB", A_W2 + 512 * (kc % 3), A_W2 + 512 * (kc % 3) + 512),
                            start=False, stop=(kc == 7)),
                            reads=[aslab_l[kc], wb], writes=[pbank[2 + tt]], accum=True)
                if g == 0 and deferred_splits:
                    sw = deferred_splits.pop(0)
                    split_weight(sw[0], sw[1], sw[2], sw[3], late=True)
                for tt in range(4):
                    if slab == 0:
                        p.do("vector", lambda e, tt=tt, nt=nt: e.tensor_copy(
                            hv("FPG", 2048 * tt + 512 * nt, 2048 * tt + 512 * (nt + 1)),
                            banks[2 + tt][:, 0:512]),
                            reads=[pbank[2 + tt]], writes=[gb])
                    else:
                        p.do("vector", lambda e, tt=tt, nt=nt: e.tensor_add(
                            hv("FPG", 2048 * tt + 512 * nt, 2048 * tt + 512 * (nt + 1)),
                            hv("FPG", 2048 * tt + 512 * nt, 2048 * tt + 512 * (nt + 1)),
                            banks[2 + tt][:, 0:512]),
                            reads=[pbank[2 + tt], gb], writes=[gb])
            if g == 0:
                for _ in range(2):
                    if gx_p1end:
                        sx, dx, semx = gx_p1end.pop(0)
                        cast_gx(sx, dx, semx)
                for _ in range(2 if slab < 5 else 3):
                    if gx_p2a:
                        sx, dx, semx = gx_p2a.pop(0)
                        cast_gx(sx, dx, semx)
                    elif gx_p2b:
                        sx, dx, semx = gx_p2b.pop(0)
                        cast_gx(sx, dx, semx)
        p.barrier("P2_%d" % g)
        if stop_here("P2_%d" % g):
            return nc

        def cq_load(seq):
            kcl = seq % 16
            sl3 = seq % 3
            _s = next_slot()
            load(bv("AB", A_CQ + 3072 * sl3, A_CQ + 3072 * sl3 + 1536),
                 cqB[0, :, 1536 * kcl:1536 * (kcl + 1)], cqbuf[sl3], slot=_s)
            load(bv("AB", A_CQ + 3072 * sl3 + 1536, A_CQ + 3072 * sl3 + 3072),
                 cqB[1, :, 1536 * kcl:1536 * (kcl + 1)], cqbuf[sl3], slot=_s)

        if g == 0:
            p.raw_wait("sync", "GXE", p.cnt.get("GXE", 0))
        cq_load(0)
        cq_load(1)
        for hm in range(2):
            for kc in range(16):
                seq = hm * 16 + kc
                sl3 = seq % 3
                wb = cqbuf[sl3]
                if seq + 2 < 32:
                    cq_load(seq + 2)
                for m6 in range(6):
                    mc = hm * 6 + m6
                    mm3w(banks[m6][:, 0:512], None,
                         bv("AB", A_CQ + 3072 * sl3 + 128 * mc, A_CQ + 3072 * sl3 + 128 * (mc + 1)),
                         bv("AB", A_CQ + 3072 * sl3 + 1536 + 128 * mc, A_CQ + 3072 * sl3 + 1536 + 128 * (mc + 1)),
                         bv("XN", 512 * kc, 512 * (kc + 1)),
                         bv("XN", 8192 + 512 * kc, 8192 + 512 * (kc + 1)),
                         [wb], [xnown], kc == 0, kc == 15, n=m6)
            for m6 in range(6):
                mc = hm * 6 + m6
                evac_pair(banks[m6][:, 0:512], m6,
                          bv("NQ", 512 * mc, 512 * (mc + 1)),
                          bv("NQ", 6144 + 512 * mc, 6144 + 512 * (mc + 1)), nqb, nqb,
                          scale=hv("consts", C_GQ + mc, C_GQ + mc + 1))
                p.do("scalar", lambda e, m6=m6: e.activation(
                    hv("FP1", F_SQ, F_SQ + 512), banks[m6][:, 0:512], AF.Square),
                    reads=[pbank[m6], nqb], writes=[rqrow])
                p.do("tensor", lambda e, mc=mc: e.matmul(
                    banks[6][0:1, 0:512], hv("consts", C_ONES, C_ONES + 1),
                    hv("FP1", F_SQ, F_SQ + 512),
                    start=(mc == 0), stop=(mc == 11)),
                    reads=[rqrow, cstb], writes=[pbank[6]], accum=(mc != 0))
        p.do("scalar", lambda e: e.activation(hs("FP1", 64, 65, F_SQ, F_SQ + 512),
                                              banks[6][0:1, 0:512],
                                              AF.Sqrt, bias=hs("consts", 64, 65, C_EPS, C_EPS + 1),
                                              scale=1.0 / R_Q),
             reads=[pbank[6], cstb], writes=[rqrow])
        p.do("vector", lambda e: e.reciprocal(hs("FP1", 64, 65, F_SQ, F_SQ + 512),
                                              hs("FP1", 64, 65, F_SQ, F_SQ + 512)),
             reads=[rqrow], writes=[rqrow])
        for i in range(4):
            p.do("tensor", lambda e, i=i: e.matmul(
                banks[7][:, 0:1], hs("FP1", 64, 65, F_SQ + 128 * i, F_SQ + 128 * (i + 1)),
                hs("consts", 64, 65, C_ONES, C_ONES + 1), start=True, stop=True),
                reads=[rqrow, cstb], writes=[pbank[7]])
            p.do("vector", lambda e, i=i: e.tensor_scalar_mul(
                hv("stats", ST_SCQ + i, ST_SCQ + i + 1), banks[7][:, 0:1], ATTN_SCALE),
                reads=[pbank[7]], writes=[scqb])
        load(hv("FPC", C_MASK, C_MASK + 2048).rearrange("p (i m) -> p i m", m=512),
             lm_d[4 * g:4 * (g + 1), :, :].rearrange("i p m -> p i m"), maskb)

        def load_kv(h, extra=()):
            par = h % 2
            kv = kvb[par]
            _s = next_slot()
            load(bv("AB", A_K + 4096 * par, A_K + 4096 * par + 2048), kkB[0, h], kv, slot=_s,
                 extra_writes=extra)
            load(bv("AB", A_K + 4096 * par + 2048, A_K + 4096 * par + 4096), kkB[1, h], kv, slot=_s,
                 extra_writes=extra)
            hg, hi4 = h // 4, h % 4
            src0 = vvB[0, hg][:, :].rearrange("p (kt m) -> p kt m", m=512)[:, :, 128 * hi4:128 * (hi4 + 1)]
            src1 = vvB[1, hg][:, :].rearrange("p (kt m) -> p kt m", m=512)[:, :, 128 * hi4:128 * (hi4 + 1)]
            load(bv("AB", A_V + 4096 * par, A_V + 4096 * par + 2048)
                 .rearrange("p (kt m) -> p kt m", m=128), src0, kv, slot=_s)
            load(bv("AB", A_V + 4096 * par + 2048, A_V + 4096 * par + 4096)
                 .rearrange("p (kt m) -> p kt m", m=128), src1, kv, slot=_s)
            load(bs("AB", 0, 64, A_KROT + 4096 * par, A_KROT + 4096 * par + 2048), krotB[0, h], kv, slot=_s)
            load(bs("AB", 0, 64, A_KROT + 4096 * par + 2048, A_KROT + 4096 * par + 4096), krotB[1, h], kv, slot=_s)

        def load_dq(h):
            _s = next_slot()
            load(bv("AB", A_DQN, A_DQN + 1536), dqnB[0, h], dqp, slot=_s)
            load(bv("AB", A_DQN + 1536, A_DQN + 3072), dqnB[1, h], dqp, slot=_s)
            load(bv("AB", A_DQR, A_DQR + 768), dqrB[0, h], dqp, slot=_s)
            load(bv("AB", A_DQR + 768, A_DQR + 1536), dqrB[1, h], dqp, slot=_s)

        if g == 0:
            p.raw_wait("sync", "GXE", p.cnt.get("GXE", 0))
        load_dq(0)
        load_kv(0, extra=(cqbuf[0], cqbuf[1], cqbuf[2]))
        p.barrier("P3_%d" % g)
        if stop_here("P3_%d" % g):
            return nc

        for h in range(H):
            par = h % 2
            kv = kvb[par]
            cos_h = hs("consts", 0, 64, C_COS + h, C_COS + h + 1)
            spm_h = hs("consts", 0, 64, C_SPM + h, C_SPM + h + 1)
            if h + 1 < H:
                load_kv(h + 1)
            for kc in range(12):
                mm3w(banks[4][:, 0:512], None,
                     bv("AB", A_DQN + 128 * kc, A_DQN + 128 * (kc + 1)),
                     bv("AB", A_DQN + 1536 + 128 * kc, A_DQN + 1536 + 128 * (kc + 1)),
                     bv("NQ", 512 * kc, 512 * (kc + 1)),
                     bv("NQ", 6144 + 512 * kc, 6144 + 512 * (kc + 1)),
                     [dqp], [nqb], kc == 0, kc == 11, n=4)
            evac_pair(banks[4][:, 0:512], 4,
                      bv("AB", A_QN, A_QN + 512), bv("AB", A_QN + 512, A_QN + 1024), qnb, qnb)
            for kc in range(12):
                mm3w(banks[5][0:64, 0:512], None,
                     bv("AB", A_DQR + 64 * kc, A_DQR + 64 * (kc + 1)),
                     bv("AB", A_DQR + 768 + 64 * kc, A_DQR + 768 + 64 * (kc + 1)),
                     bv("NQ", 512 * kc, 512 * (kc + 1)),
                     bv("NQ", 6144 + 512 * kc, 6144 + 512 * (kc + 1)),
                     [dqp], [nqb], kc == 0, kc == 11, n=5)
            if h + 1 < H:
                load_dq(h + 1)
            p.do("vector", lambda e: e.tensor_copy(hs("FP1", 0, 64, F_QRF, F_QRF + 512),
                                                   banks[5][0:64, 0:512]),
                 reads=[pbank[5]], writes=[qrf])
            p.do("tensor", lambda e: e.matmul(
                banks[6][0:64, 0:512], swap64, hs("FP1", 0, 64, F_QRF, F_QRF + 512),
                start=True, stop=True),
                reads=[qrf, cstb], writes=[pbank[6]])
            p.do("vector", lambda e: e.tensor_copy(hs("FP1", 0, 64, F_QSF, F_QSF + 512),
                                                   banks[6][0:64, 0:512]),
                 reads=[pbank[6]], writes=[qrf])
            p.do("vector", lambda e, cos_h=cos_h: e.tensor_scalar_mul(
                hs("FP1", 0, 64, F_QOF, F_QOF + 512), hs("FP1", 0, 64, F_QRF, F_QRF + 512), cos_h),
                reads=[qrf, cstb], writes=[qrotb])
            p.do("vector", lambda e, spm_h=spm_h: e.scalar_tensor_tensor(
                hs("FP1", 0, 64, F_QOF, F_QOF + 512), hs("FP1", 0, 64, F_QSF, F_QSF + 512), spm_h,
                hs("FP1", 0, 64, F_QOF, F_QOF + 512), ALU.mult, ALU.add),
                reads=[qrf, cstb, qrotb], writes=[qrotb])
            p.do("scalar", lambda e: e.activation(
                bs("AB", 0, 64, A_QROT, A_QROT + 512), hs("FP1", 0, 64, F_QOF, F_QOF + 512), AF.Copy),
                reads=[qrotb], writes=[qrotb])
            p.do("vector", lambda e: e.tensor_tensor(
                bs("AB", 0, 64, A_QROT + 512, A_QROT + 1024), hs("FP1", 0, 64, F_QOF, F_QOF + 512),
                bs("AB", 0, 64, A_QROT, A_QROT + 512), ALU.subtract),
                reads=[qrotb], writes=[qrotb])
            def emit_scores(i):
                s_i = svals[i]
                for st in range(s_i):
                    bk = 4 + st
                    qn_h = bv("AB", A_QN + 128 * i, A_QN + 128 * (i + 1))
                    qn_l = bv("AB", A_QN + 512 + 128 * i, A_QN + 512 + 128 * (i + 1))
                    kn_h = bv("AB", A_K + 4096 * par + 512 * st, A_K + 4096 * par + 512 * (st + 1))
                    kn_l = bv("AB", A_K + 4096 * par + 2048 + 512 * st, A_K + 4096 * par + 2048 + 512 * (st + 1))
                    qr_h = bs("AB", 0, 64, A_QROT + 128 * i, A_QROT + 128 * (i + 1))
                    qr_l = bs("AB", 0, 64, A_QROT + 512 + 128 * i, A_QROT + 512 + 128 * (i + 1))
                    kr_h = bs("AB", 0, 64, A_KROT + 4096 * par + 512 * st, A_KROT + 4096 * par + 512 * (st + 1))
                    kr_l = bs("AB", 0, 64, A_KROT + 4096 * par + 2048 + 512 * st, A_KROT + 4096 * par + 2048 + 512 * (st + 1))
                    p.do("tensor", lambda e, o=banks[bk][:, 0:512], a=qn_h, b=kn_h: e.matmul(
                        o, a, b, start=True, stop=False),
                        reads=[qnb, kv], writes=[pbank[bk]])
                    p.do("tensor", lambda e, o=banks[bk][:, 0:512], a=qn_h, b=kn_l: e.matmul(
                        o, a, b, start=False, stop=False),
                        reads=[qnb, kv], writes=[pbank[bk]], accum=True)
                    p.do("tensor", lambda e, o=banks[bk][:, 0:512], a=qn_l, b=kn_h: e.matmul(
                        o, a, b, start=False, stop=False),
                        reads=[qnb, kv], writes=[pbank[bk]], accum=True)
                    p.do("tensor", lambda e, o=banks[bk][:, 0:512], a=qr_h, b=kr_h: e.matmul(
                        o, a, b, start=False, stop=False),
                        reads=[qrotb, kv], writes=[pbank[bk]], accum=True)
                    p.do("tensor", lambda e, o=banks[bk][:, 0:512], a=qr_h, b=kr_l: e.matmul(
                        o, a, b, start=False, stop=False),
                        reads=[qrotb, kv], writes=[pbank[bk]], accum=True)
                    p.do("tensor", lambda e, o=banks[bk][:, 0:512], a=qr_l, b=kr_h: e.matmul(
                        o, a, b, start=False, stop=True),
                        reads=[qrotb, kv], writes=[pbank[bk]], accum=True)
                bklast = 4 + s_i - 1
                p.do("vector", lambda e, i=i, bklast=bklast: e.tensor_add(
                    banks[bklast][:, 0:512], banks[bklast][:, 0:512],
                    hv("FPC", C_MASK + 512 * i, C_MASK + 512 * (i + 1))),
                    reads=[pbank[bklast], maskb], writes=[pbank[bklast]])

            def emit_softmax(i):
                s_i = svals[i]
                for st in range(s_i):
                    p.do("scalar", lambda e, i=i, st=st: e.activation(
                        hv("FP1", F_PROBS + 512 * st, F_PROBS + 512 * (st + 1)),
                        banks[4 + st][:, 0:512], AF.Exp,
                        bias=hv("consts", C_M30, C_M30 + 1),
                        scale=hv("stats", ST_SCQ + i, ST_SCQ + i + 1),
                        accum_out=hv("stats", ST_DEN + st, ST_DEN + st + 1)),
                        reads=[pbank[4 + st], cstb, scqb], writes=[probs, statb])
                p.do("vector", lambda e, s_i=s_i: e.reduce_sum(
                    hv("stats", ST_RC, ST_RC + 1), hv("stats", ST_DEN, ST_DEN + s_i), axis=AX.X),
                    reads=[statb], writes=[statb])
                p.do("vector", lambda e: e.reciprocal(
                    hv("stats", ST_RC, ST_RC + 1), hv("stats", ST_RC, ST_RC + 1)),
                    reads=[statb], writes=[statb])
                p.do("vector", lambda e, s_i=s_i: e.tensor_scalar_mul(
                    hv("FP1", F_PROBS, F_PROBS + 512 * s_i), hv("FP1", F_PROBS, F_PROBS + 512 * s_i),
                    hv("stats", ST_RC, ST_RC + 1)),
                    reads=[probs, statb], writes=[probs])

            def emit_pv(i):
                s_i = svals[i]
                nkc = 4 * s_i

                def pv_transpose(kc):
                    bt = 1 + kc % 3
                    p.do("tensor", lambda e, kc=kc, bt=bt: e.transpose(
                        banks[bt][:, 0:128], hv("FP1", F_PROBS + 128 * kc, F_PROBS + 128 * (kc + 1)),
                        ident),
                        reads=[probs, cstb], writes=[pbank[bt]])
                    evac_pair(banks[bt][:, 0:128], bt,
                              bv("AB", A_PT + 256 * (kc % 3), A_PT + 256 * (kc % 3) + 128),
                              bv("AB", A_PT + 256 * (kc % 3) + 128, A_PT + 256 * (kc % 3) + 256),
                              ptb[kc % 3], ptb[kc % 3])

                for kc in range(min(2, nkc)):
                    pv_transpose(kc)
                for kc in range(nkc):
                    if kc + 2 < nkc:
                        pv_transpose(kc + 2)
                    v_h = bv("AB", A_V + 4096 * par + 128 * kc, A_V + 4096 * par + 128 * (kc + 1))
                    v_l = bv("AB", A_V + 4096 * par + 2048 + 128 * kc, A_V + 4096 * par + 2048 + 128 * (kc + 1))
                    t_h = bv("AB", A_PT + 256 * (kc % 3), A_PT + 256 * (kc % 3) + 128)
                    t_l = bv("AB", A_PT + 256 * (kc % 3) + 128, A_PT + 256 * (kc % 3) + 256)
                    p.do("tensor", lambda e, o=banks[0][:, 0:128], a=v_h, b=t_h, kc=kc: e.matmul(
                        o, a, b, start=(kc == 0), stop=False),
                        reads=[kv, ptb[kc % 3]], writes=[pbank[0]], accum=(kc != 0))
                    p.do("tensor", lambda e, o=banks[0][:, 0:128], a=v_h, b=t_l: e.matmul(
                        o, a, b, start=False, stop=False),
                        reads=[kv, ptb[kc % 3]], writes=[pbank[0]], accum=True)
                    p.do("tensor", lambda e, o=banks[0][:, 0:128], a=v_l, b=t_h, kc=kc, nkc=nkc: e.matmul(
                        o, a, b, start=False, stop=(kc == nkc - 1)),
                        reads=[kv, ptb[kc % 3]], writes=[pbank[0]], accum=True)
                evac_pair(banks[0][:, 0:128], 0,
                          bv("XN", 512 * h + 128 * i, 512 * h + 128 * (i + 1)),
                          bv("XN", 8192 + 512 * h + 128 * i, 8192 + 512 * h + 128 * (i + 1)),
                          outfm, outfm)

            emit_scores(0)
            for i in range(4):
                emit_softmax(i)
                if i + 1 < 4:
                    emit_scores(i + 1)
                emit_pv(i)
        if g == 0:
            p.raw_wait("sync", "GXF", p.cnt.get("GXF", 0))
        _s = next_slot()
        load(bv("AB", A_PR, A_PR + 2048), prB[0, :, 0:2048], prbuf[0], slot=_s,
             extra_writes=(kvb[0],))
        load(bv("AB", A_PR + 2048, A_PR + 4096), prB[1, :, 0:2048], prbuf[0], slot=_s,
             extra_writes=(kvb[0],))
        _s = next_slot()
        load(bv("AB", A_PR + 4096, A_PR + 6144), prB[0, :, 2048:4096], prbuf[1], slot=_s,
             extra_writes=(kvb[1],))
        load(bv("AB", A_PR + 6144, A_PR + 8192), prB[1, :, 2048:4096], prbuf[1], slot=_s,
             extra_writes=(kvb[1],))
        p.barrier("P4end_%d" % g)
        if stop_here("P4_%d" % g):
            return nc

        def pr_load(seq):
            kcl = seq % 16
            sl3 = seq % 3
            _s = next_slot()
            load(bv("AB", A_PR + 4096 * sl3, A_PR + 4096 * sl3 + 2048),
                 prB[0, :, 2048 * kcl:2048 * (kcl + 1)], prbuf[sl3], slot=_s)
            load(bv("AB", A_PR + 4096 * sl3 + 2048, A_PR + 4096 * sl3 + 4096),
                 prB[1, :, 2048 * kcl:2048 * (kcl + 1)], prbuf[sl3], slot=_s)


        for tp in range(2):
            for kc in range(16):
                seq = tp * 16 + kc
                sl3 = seq % 3
                wb = prbuf[sl3]
                if seq + 2 < 32:
                    pr_load(seq + 2)
                for tt2 in range(2):
                    for nt in range(4):
                        o_h = bv("XN", 512 * kc + 128 * (tp * 2 + tt2), 512 * kc + 128 * (tp * 2 + tt2) + 128)
                        o_l = bv("XN", 8192 + 512 * kc + 128 * (tp * 2 + tt2), 8192 + 512 * kc + 128 * (tp * 2 + tt2) + 128)
                        p_h = bv("AB", A_PR + 4096 * sl3 + 512 * nt, A_PR + 4096 * sl3 + 512 * (nt + 1))
                        p_l = bv("AB", A_PR + 4096 * sl3 + 2048 + 512 * nt, A_PR + 4096 * sl3 + 2048 + 512 * (nt + 1))
                        nb = tt2 * 4 + nt
                        p.do("tensor", lambda e, o=banks[nb][:, 0:512], a=o_h, b=p_h, kc=kc: e.matmul(
                            o, a, b, start=(kc == 0), stop=False),
                            reads=[outfm, wb], writes=[pbank[nb]], accum=(kc != 0))
                        p.do("tensor", lambda e, o=banks[nb][:, 0:512], a=o_l, b=p_h: e.matmul(
                            o, a, b, start=False, stop=False),
                            reads=[outfm, wb], writes=[pbank[nb]], accum=True)
                        p.do("tensor", lambda e, o=banks[nb][:, 0:512], a=o_h, b=p_l, kc=kc: e.matmul(
                            o, a, b, start=False, stop=(kc == 15)),
                            reads=[outfm, wb], writes=[pbank[nb]], accum=True)
            for tt2 in range(2):
                tt = tp * 2 + tt2
                sl = g * 4 + tt
                load(hv("FPC", C_XTOK, C_XTOK + 2048), xq_d[128 * sl:128 * (sl + 1), :], xtok)
                for nt in range(4):
                    nb = tt2 * 4 + nt
                    p.do("scalar", lambda e, tt=tt, nt=nt: e.activation(
                        hv("FP1", F_PROBS, F_PROBS + 512),
                        hv("FPG", 2048 * tt + 512 * nt, 2048 * tt + 512 * (nt + 1)), AF.Silu),
                        reads=[gb], writes=[sgb])
                    p.do("vector", lambda e, nb=nb: e.tensor_mul(
                        hv("FP1", F_PROBS + 512, F_PROBS + 1024), banks[nb][:, 0:512],
                        hv("FP1", F_PROBS, F_PROBS + 512)),
                        reads=[pbank[nb], sgb], writes=[sgb])
                    p.do("vector", lambda e, nt=nt: e.tensor_add(
                        hv("FP1", F_PROBS + 1024, F_PROBS + 1536),
                        hv("FP1", F_PROBS + 512, F_PROBS + 1024),
                        hv("FPC", C_XTOK + 512 * nt, C_XTOK + 512 * (nt + 1))),
                        reads=[sgb, xtok], writes=[obuf])
                    store(out_d[128 * sl:128 * (sl + 1), 512 * nt:512 * (nt + 1)],
                          hv("FP1", F_PROBS + 1024, F_PROBS + 1536), obuf)
        p.barrier("P5_%d" % g)

    for sn in list(p.cnt):
        if sn.startswith("ST"):
            p.raw_wait("gpsimd", sn, p.cnt.get(sn, 0))
        if sn.startswith("SY"):
            p.raw_wait("sync", sn, p.cnt.get(sn, 0))
    with nc.Block() as block:
        p.emit(block)
    return nc


def _stage_inputs(L0_attn_x, L0_attn_freqs_cos, L0_attn_freqs_sin, L0_attn_w_cq,
                  L0_attn_q_norm_w, L0_attn_w_dq_nope, L0_attn_w_dq_rope, L0_attn_w_ckv,
                  L0_attn_kv_norm_w, L0_attn_w_dk_nope, L0_attn_w_dv, L0_attn_w_k_rope,
                  L0_attn_w_proj, L0_ffn_W1, L0_ffn_W2):
    f32 = np.float32
    x = np.asarray(L0_attn_x, f32)

    def rope_perm(w, nh):
        w = np.asarray(w, f32)
        d = w.shape[1] // nh
        cols = []
        for h in range(nh):
            base = h * d
            cols += [base + 2 * j for j in range(d // 2)]
            cols += [base + 2 * j + 1 for j in range(d // 2)]
        return np.ascontiguousarray(w[:, cols])
    w_dqr = rope_perm(L0_attn_w_dq_rope, H)
    w_kr = rope_perm(L0_attn_w_k_rope, 1)

    consts = np.zeros((128, 576), f32)
    consts[:, 0:128] = np.eye(128, dtype=f32)
    consts[:, 128:256] = 1.0
    sw = np.zeros((128, 128), f32)
    for i in range(64):
        sw[(i + 32) % 64, i] = 1.0
    consts[:, 256:384] = sw
    consts[:, 384:396] = np.asarray(L0_attn_q_norm_w, f32).reshape(12, 128).T
    consts[:, 396:400] = np.asarray(L0_attn_kv_norm_w, f32).reshape(4, 128).T
    cos = np.asarray(L0_attn_freqs_cos, f32)
    sin = np.asarray(L0_attn_freqs_sin, f32)
    for h in range(16):
        consts[0:32, 400 + h] = cos[h]
        consts[32:64, 400 + h] = cos[h]
        consts[0:32, 416 + h] = sin[h]
        consts[32:64, 416 + h] = sin[h]
    consts[0:32, 432] = -1.0
    consts[32:64, 432] = 1.0
    consts[:, 449] = 1e-5
    consts[:, 450] = -30.0

    def chunk_heads(w, nh, dm):
        K = w.shape[0]
        nkc = K // 128
        a = w.reshape(nkc, 128, nh, dm).transpose(2, 1, 0, 3).reshape(nh, 128, nkc * dm)
        return np.ascontiguousarray(a)

    w_dqn = np.asarray(L0_attn_w_dq_nope, f32)
    w_dkn = np.asarray(L0_attn_w_dk_nope, f32)
    w_dv = np.asarray(L0_attn_w_dv, f32)
    w_ckv = np.asarray(L0_attn_w_ckv, f32)
    W1 = np.asarray(L0_ffn_W1, f32)

    dv4 = w_dv.reshape(4, 128, 4, 4, 128)
    dv4 = dv4.transpose(2, 1, 0, 3, 4).reshape(4, 128, 4 * 512)

    shared = {
        "consts": consts,
        "wcqs": np.ascontiguousarray(np.asarray(L0_attn_w_cq, f32)
                                     .reshape(16, 128, 1536).transpose(1, 0, 2)
                                     .reshape(128, 16 * 1536)),
        "wdqns": chunk_heads(w_dqn, H, 128),
        "wdqrs": chunk_heads(w_dqr, H, 64),
        "wckvs": np.ascontiguousarray(
            w_ckv.reshape(16, 128, 512).transpose(1, 0, 2).reshape(128, 16 * 512)),
        "wdkns": chunk_heads(w_dkn, H, 128),
        "wdvs4": np.ascontiguousarray(dv4),
        "wkrs": np.ascontiguousarray(
            w_kr.reshape(16, 128, 64).transpose(1, 0, 2).reshape(128, 16 * 64)),
        "wprs": np.ascontiguousarray(np.asarray(L0_attn_w_proj, f32)
                                     .reshape(16, 128, 2048).transpose(1, 0, 2)
                                     .reshape(128, 16 * 2048)),
        "w1s": np.ascontiguousarray(
            W1.T.reshape(16, 128, 64, 128).transpose(2, 1, 0, 3).reshape(64, 128, 16 * 128)),
        "w2t": np.ascontiguousarray(np.asarray(L0_ffn_W2, f32).T),
    }
    in_maps = []
    for c in range(8):
        b, half = c // 2, c % 2
        slots = OWN[half]
        xq = np.ascontiguousarray(
            np.concatenate([x[b, 128 * gt:128 * (gt + 1), :] for gt in slots], axis=0))
        lm = np.zeros((8, 128, 512), f32)
        for i, gt in enumerate(slots):
            s_i = SLIST[i]
            keys = (s_i - 1) * 512 + np.arange(512)
            qpos = gt * 128 + np.arange(128)
            lm[i] = np.where(keys[None, :] <= qpos[:, None], 0.0, NEG)
        m = dict(shared)
        m["xq"] = xq
        m["xnat"] = np.ascontiguousarray(x[b])
        m["lastmask"] = lm
        in_maps.append(m)
    return in_maps


def kernel(L0_attn_x, L0_attn_mask, L0_attn_freqs_cos, L0_attn_freqs_sin, L0_attn_w_cq,
           L0_attn_q_norm_w, L0_attn_w_dq_nope, L0_attn_w_dq_rope, L0_attn_w_ckv,
           L0_attn_kv_norm_w, L0_attn_w_dk_nope, L0_attn_w_dv, L0_attn_w_k_rope,
           L0_attn_w_proj, L0_ffn_W1, L0_ffn_W2):
    in_maps = _stage_inputs(L0_attn_x, L0_attn_freqs_cos, L0_attn_freqs_sin, L0_attn_w_cq,
                            L0_attn_q_norm_w, L0_attn_w_dq_nope, L0_attn_w_dq_rope,
                            L0_attn_w_ckv, L0_attn_kv_norm_w, L0_attn_w_dk_nope,
                            L0_attn_w_dv, L0_attn_w_k_rope, L0_attn_w_proj,
                            L0_ffn_W1, L0_ffn_W2)
    nc = build_program()
    res = run_bass_kernel_spmd(nc, in_maps, core_ids=list(range(8)))
    out = np.empty((B, S, D), np.float32)
    for c in range(8):
        b, half = c // 2, c % 2
        r = np.asarray(res.results[c]["out"])
        for i, gt in enumerate(OWN[half]):
            out[b, 128 * gt:128 * (gt + 1), :] = r[128 * i:128 * (i + 1), :]
    return out
